# version 07
# baseline (speedup 1.0000x reference)
import numpy as np
import ml_dtypes
from contextlib import ExitStack

import concourse.bass as bass
import concourse.mybir as mybir
from concourse.bass_utils import run_bass_kernel_spmd

F32 = mybir.dt.float32
BF16 = mybir.dt.bfloat16
I32 = mybir.dt.int32
AF = mybir.ActivationFunctionType
ALU = mybir.AluOpType
AX = mybir.AxisListType

D = 2048
T = 1024
H = 8192
NDC = D // 128
NHH = 8
HH = H // NHH
NCORES = 8

INV_D = 1.0 / D
INV127 = 1.0 / 127.0


def build_nc():
    nc = bass.Bass("TRN2", target_bir_lowering=False)

    xt = nc.dram_tensor("xt", [D, T], F32, kind="ExternalInput").ap()
    xw_in = nc.dram_tensor("xw", [D, 128], F32, kind="ExternalInput").ap()
    wft = nc.dram_tensor("wft", [D, D], F32, kind="ExternalInput").ap()
    wit = nc.dram_tensor("wit", [D, D], F32, kind="ExternalInput").ap()
    wgt = nc.dram_tensor("wgt", [D, D], F32, kind="ExternalInput").ap()
    wot = nc.dram_tensor("wot", [D, D], F32, kind="ExternalInput").ap()
    wgq = nc.dram_tensor("wgq", [NHH, D, HH], F32, kind="ExternalInput").ap()
    wuq = nc.dram_tensor("wuq", [NHH, D, HH], F32, kind="ExternalInput").ap()
    wdt = nc.dram_tensor("wdt", [H, D], F32, kind="ExternalInput").ap()
    wn_in = nc.dram_tensor("wn", [128, NDC], F32, kind="ExternalInput").ap()
    onec_in = nc.dram_tensor("onec", [128, 1], BF16, kind="ExternalInput").ap()
    oner_in = nc.dram_tensor("oner", [1, 128], F32, kind="ExternalInput").ap()
    ident_in = nc.dram_tensor("ident", [128, 128], F32, kind="ExternalInput").ap()
    consts_in = nc.dram_tensor("consts", [128, 4], F32, kind="ExternalInput").ap()
    x2sp = nc.dram_tensor("x2sp", [D, T], F32, kind="Internal").ap()
    out = nc.dram_tensor("out", [T, D], F32, kind="ExternalOutput").ap()

    sem_counts = {}
    blocks = []
    cur = []

    def new_block():
        nonlocal cur
        cur = []
        blocks.append(cur)

    def op(eng, fn, inc=None, waits=()):
        waits = [(s, v) for (s, v) in waits if v > 0]
        cur.append((eng, waits, fn, inc))
        if inc is not None:
            sem_counts[inc[0]] = sem_counts.get(inc[0], 0) + inc[1]

    def cnt(sem):
        return sem_counts.get(sem, 0)

    ctx = ExitStack()
    with ctx:
        semnames = ["w0", "w1", "gd", "o0", "o1", "o2", "te", "ve", "se"]
        sems = {n: ctx.enter_context(nc.semaphore(n)) for n in semnames}
        for n in semnames:
            sem_counts[n] = 0

        B1 = ctx.enter_context(nc.sbuf_tensor("B1", [128, NDC * T], BF16))
        STG = [ctx.enter_context(nc.sbuf_tensor(f"STG{i}", [128, 2048], F32))
               for i in range(2)]
        ROWS = ctx.enter_context(nc.sbuf_tensor("ROWS", [1, 2 * T], F32))
        SCRA = ctx.enter_context(nc.sbuf_tensor("SCRA", [128, 3 * T], F32))
        SCB = ctx.enter_context(nc.sbuf_tensor("SCB", [128, 2 * T], BF16))
        WNS = ctx.enter_context(nc.sbuf_tensor("WNS", [128, NDC], F32))
        ONEC = ctx.enter_context(nc.sbuf_tensor("ONEC", [128, 1], BF16))
        ONER = ctx.enter_context(nc.sbuf_tensor("ONER", [1, 128], F32))
        IDENT = ctx.enter_context(nc.sbuf_tensor("IDENT", [128, 128], F32))
        CONSTS = ctx.enter_context(nc.sbuf_tensor("CONSTS", [128, 4], F32))
        PS = [ctx.enter_context(nc.psum_tensor(f"PS{i}", [128, 1024], F32))
              for i in range(4)]

        def b1(dc):
            return B1[:, dc * T:(dc + 1) * T]

        R0 = ROWS[:, 0:T]
        R1 = ROWS[:, T:2 * T]
        SCROW = SCRA[0:1, 0:T]

        def scra(i):
            return SCRA[:, i * T:(i + 1) * T]

        c_eps5 = CONSTS[0:1, 1:2]
        c_eps8 = CONSTS[0:1, 2:3]

        ring = {"idx": 0, "fill": [0, 0], "war": [None, None]}

        def ring_load(dma_fn, extra_waits=()):
            i = ring["idx"] % 2
            ring["idx"] += 1
            waits = list(extra_waits)
            if ring["war"][i] is not None:
                waits.append(ring["war"][i])
            op("sync", (lambda e, i=i: dma_fn(e, STG[i])),
               inc=(f"w{i}", 16), waits=waits)
            ring["fill"][i] += 1
            return i

        def ring_wait(i):
            return (f"w{i}", 16 * ring["fill"][i])

        def ring_done(i, eng):
            ring["war"][i] = (eng, cnt(eng))

        ps_free = [None, None, None, None]

        def mm_waits(psi, extra):
            w = list(extra)
            if ps_free[psi] is not None:
                w.append(ps_free[psi])
            return w

        def bcast(row_ap, psi, dst, ready):
            def bmm(e, row_ap=row_ap, psi=psi):
                e.matmul(PS[psi][:, 0:512], ONER[:], row_ap[:, 0:512],
                         start=True, stop=True)
                return e.matmul(PS[psi][:, 512:1024], ONER[:],
                                row_ap[:, 512:1024], start=True, stop=True)
            op("tensor", bmm, inc=("te", 1),
               waits=mm_waits(psi, [ready, ("w0", 80)]))
            op("vector", lambda e, psi=psi, dst=dst: e.tensor_copy(dst, PS[psi][:]),
               inc=("ve", 1), waits=[("te", cnt("te"))])
            ps_free[psi] = ("ve", cnt("ve"))

        wr_ctx = ExitStack()
        WR = wr_ctx.enter_context(
            nc.sbuf_tensor("WR", [128, NDC * 2048], BF16, side="right"))
        bb_ctx = ExitStack()
        B2 = bb_ctx.enter_context(
            nc.sbuf_tensor("B2", [128, NDC * T], BF16, side="right"))
        B3 = bb_ctx.enter_context(
            nc.sbuf_tensor("B3", [128, NDC * T], BF16, side="right"))
        FWB = bb_ctx.enter_context(
            nc.sbuf_tensor("FWB", [128, NDC * 128], BF16, side="right"))
        HINC = bb_ctx.enter_context(
            nc.sbuf_tensor("HINC", [128, NDC], F32, side="right"))

        def b2(dc):
            return B2[:, dc * T:(dc + 1) * T]

        def b3(dc):
            return B3[:, dc * T:(dc + 1) * T]

        def wr(dc):
            return WR[:, dc * 2048:(dc + 1) * 2048]

        new_block()

        op("sync", lambda e: e.dma_start(out=WNS[:], in_=wn_in[:, :]), inc=("w0", 16))
        op("sync", lambda e: e.dma_start(out=ONEC[:], in_=onec_in[:, :]), inc=("w0", 16))
        op("sync", lambda e: e.dma_start(out=ONER[:], in_=oner_in[:, :]), inc=("w0", 16))
        op("sync", lambda e: e.dma_start(out=IDENT[:], in_=ident_in[:, :]), inc=("w0", 16))
        op("sync", lambda e: e.dma_start(out=CONSTS[:], in_=consts_in[:, :]), inc=("w0", 16))
        ring["fill"][0] += 5

        for u in range(8):
            si = ring_load(lambda e, t, u=u: e.dma_start(
                out=t[:].rearrange("p (j t) -> p j t", j=2),
                in_=xt[u * 256:(u + 1) * 256, :].rearrange("(j p) t -> p j t", p=128)))
            for j in range(2):
                dc = 2 * u + j
                op("vector", lambda e, si=si, j=j, dc=dc: e.tensor_copy(
                    b1(dc), STG[si][:, j * 1024:(j + 1) * 1024]),
                    inc=("ve", 1), waits=[ring_wait(si)])
            ring_done(si, "ve")
        si = ring_load(lambda e, t: e.dma_start(
            out=t[:].rearrange("p (j w) -> p j w", j=16),
            in_=xw_in[:, :].rearrange("(j p) w -> p j w", p=128)))
        op("vector", lambda e, si=si: e.tensor_copy(SCB[:], STG[si][:]),
           inc=("ve", 1), waits=[ring_wait(si)])
        ring_done(si, "ve")
        x_cast_done = cnt("ve")

        def xw_bf(dc):
            return SCB[:, dc * 128:(dc + 1) * 128]

        def window_proj(evac_fn, wready):
            for ec in range(NDC):
                psi = ec % 4
                w = mm_waits(psi, [("ve", x_cast_done), ("ve", wready)])

                def wmms(e, ec=ec, psi=psi):
                    inst = None
                    for dc in range(NDC):
                        lhsT = wr(dc)[:, ec * 128:(ec + 1) * 128]
                        inst = e.matmul(PS[psi][:, 0:128], lhsT, xw_bf(dc),
                                        start=(dc == 0), stop=(dc == NDC - 1))
                    return inst
                op("tensor", wmms, inc=("te", 1), waits=w)
                evac_fn(ec, psi, cnt("te"))

        def stream_attn_weight(whbm):
            wr_last_use = cnt("te")
            for dc in range(NDC):
                si = ring_load(lambda e, t, dc=dc, whbm=whbm: e.dma_start(
                    out=t[:], in_=whbm[dc * 128:(dc + 1) * 128, :]))
                op("vector", lambda e, si=si, dc=dc: e.tensor_copy(wr(dc), STG[si][:]),
                   inc=("ve", 1), waits=[ring_wait(si), ("te", wr_last_use)])
                ring_done(si, "ve")
            return cnt("ve")

        def attn_proj(mov_fn, evac_fn, wready, mov_ready):
            for ec in range(NDC):
                psi = ec % 4
                w = mm_waits(psi, [("ve", wready), ("ve", mov_ready)])

                def mms(e, ec=ec, psi=psi, mov_fn=mov_fn):
                    inst = None
                    for dc in range(NDC):
                        lhsT = wr(dc)[:, ec * 128:(ec + 1) * 128]
                        st, sp = (dc == 0), (dc == NDC - 1)
                        e.matmul(PS[psi][:, 0:512], lhsT, mov_fn(dc)[:, 0:512],
                                 start=st, stop=sp)
                        inst = e.matmul(PS[psi][:, 512:1024], lhsT,
                                        mov_fn(dc)[:, 512:1024], start=st, stop=sp)
                    return inst
                op("tensor", mms, inc=("te", 1), waits=w)
                evac_fn(ec, psi, cnt("te"))

        f_wready = stream_attn_weight(wft)

        def fw(ec):
            return FWB[:, ec * 128:(ec + 1) * 128]

        def fw_evac(ec, psi, tec):
            op("scalar", lambda e, ec=ec, psi=psi: e.copy(fw(ec), PS[psi][:, 0:128]),
               inc=("se", 1), waits=[("te", tec)])
            ps_free[psi] = ("se", cnt("se"))
        window_proj(fw_evac, f_wready)

        def f_evac(ec, psi, tec):
            op("scalar", lambda e, ec=ec, psi=psi: e.copy(b2(ec), PS[psi][:]),
               inc=("se", 1), waits=[("te", tec)])
            ps_free[psi] = ("se", cnt("se"))
        attn_proj(b1, f_evac, f_wready, x_cast_done)

        i_wready = stream_attn_weight(wit)

        def iw_evac(ec, psi, tec):
            s0 = scra(0)
            op("scalar", lambda e, psi=psi, s0=s0: e.activation(
                s0[:, 0:128], PS[psi][:, 0:128], AF.Silu),
                inc=("se", 1), waits=[("te", tec), ("ve", cnt("ve"))])
            ps_free[psi] = ("se", cnt("se"))
            op("scalar", lambda e, ec=ec, s0=s0: e.activation(
                s0[:, 128:256], fw(ec), AF.Sigmoid, scale=-1.0), inc=("se", 1))
            op("scalar", lambda e, ec=ec, s0=s0: e.activation(
                s0[:, 384:512], fw(ec), AF.Sigmoid), inc=("se", 1))
            op("vector", lambda e, s0=s0: e.tensor_mul(
                s0[:, 256:384], s0[:, 0:128], s0[:, 128:256]),
                inc=("ve", 1), waits=[("se", cnt("se"))])
            op("vector", lambda e, s0=s0: e.tensor_tensor_scan(
                s0[:, 512:640], s0[:, 384:512], s0[:, 256:384], 0.0,
                ALU.mult, ALU.add), inc=("ve", 1))
            op("scalar", lambda e, ec=ec, s0=s0: e.copy(
                HINC[:, ec:ec + 1], s0[:, 639:640]),
                inc=("se", 1), waits=[("ve", cnt("ve"))])
        window_proj(iw_evac, i_wready)

        def i_evac(ec, psi, tec):
            op("scalar", lambda e, psi=psi: e.activation(
                scra(0), PS[psi][:], AF.Silu),
                inc=("se", 1), waits=[("te", tec), ("ve", cnt("ve"))])
            ps_free[psi] = ("se", cnt("se"))
            op("scalar", lambda e, ec=ec: e.activation(
                scra(1), b2(ec), AF.Sigmoid, scale=-1.0), inc=("se", 1))
            op("vector", lambda e, ec=ec: e.tensor_mul(b3(ec), scra(0), scra(1)),
               inc=("ve", 1), waits=[("se", cnt("se"))])
        attn_proj(b1, i_evac, i_wready, x_cast_done)
        b_done = cnt("ve")

        g_wready = stream_attn_weight(wgt)

        for dc in range(NDC):
            op("scalar", lambda e, dc=dc: e.activation(scra(0), b2(dc), AF.Sigmoid),
               inc=("se", 1), waits=[("ve", cnt("ve") if dc else b_done)])
            op("vector", lambda e, dc=dc: e.tensor_copy(scra(1), b3(dc)),
               inc=("ve", 1), waits=[("ve", b_done)])
            op("scalar", lambda e, dc=dc: e.activation(
                scra(1)[:, 0:1], scra(0)[:, 0:1], AF.Identity,
                bias=scra(1)[:, 0:1], scale=HINC[:, dc:dc + 1]),
                inc=("se", 1), waits=[("ve", cnt("ve"))])
            op("vector", lambda e, dc=dc: e.tensor_tensor_scan(
                b2(dc), scra(0), scra(1), 0.0, ALU.mult, ALU.add),
                inc=("ve", 1), waits=[("se", cnt("se"))])
        scan_done = cnt("ve")

        def g_evac(ec, psi, tec):
            op("scalar", lambda e, ec=ec, psi=psi: e.activation(
                b3(ec), PS[psi][:], AF.Silu),
                inc=("se", 1), waits=[("te", tec), ("ve", scan_done)])
            ps_free[psi] = ("se", cnt("se"))
        attn_proj(b1, g_evac, g_wready, x_cast_done)
        g_mm_done = cnt("te")
        sg_done = cnt("se")

        o_wready = stream_attn_weight(wot)

        for dc in range(NDC):
            op("vector", lambda e, dc=dc: e.tensor_mul(b1(dc), b2(dc), b3(dc)),
               inc=("ve", 1), waits=[("se", sg_done), ("te", g_mm_done)])
            op("scalar", lambda e, dc=dc: e.activation(
                SCB[:, 0:T], b1(dc), AF.Square), inc=("se", 1),
                waits=[("ve", cnt("ve")), ("te", cnt("te"))])

            def sqmm(e, dc=dc):
                e.matmul(PS[0][0:1, 0:512], ONEC[:], SCB[:, 0:512],
                         start=(dc == 0), stop=(dc == NDC - 1))
                return e.matmul(PS[0][0:1, 512:1024], ONEC[:], SCB[:, 512:1024],
                                start=(dc == 0), stop=(dc == NDC - 1))
            w = [("se", cnt("se")), ("w0", 80)]
            if dc == 0 and ps_free[0] is not None:
                w.append(ps_free[0])
            op("tensor", sqmm, inc=("te", 1), waits=w)
        op("vector", lambda e: e.tensor_single_scalar(
            R1, PS[0][0:1, :], INV_D, ALU.mult),
            inc=("ve", 1), waits=[("te", cnt("te"))])
        ps_free[0] = ("ve", cnt("ve"))
        op("scalar", lambda e: e.activation(R1, R1, AF.Sqrt, bias=c_eps5),
           inc=("se", 1), waits=[("ve", cnt("ve")), ("w0", 80)])
        op("vector", lambda e: e.reciprocal(R0, R1),
           inc=("ve", 1), waits=[("se", cnt("se"))])
        bcast(R0, 1, scra(2), ("ve", cnt("ve")))
        rstdb_done = cnt("ve")

        for dc in range(NDC):
            op("vector", lambda e, dc=dc: e.tensor_mul(scra(0), b1(dc), scra(2)),
               inc=("ve", 1), waits=[("ve", rstdb_done), ("se", cnt("se"))])
            op("vector", lambda e, dc=dc: e.tensor_single_scalar(
                b1(dc), scra(0), WNS[:, dc:dc + 1], ALU.mult), inc=("ve", 1))
        y_done = cnt("ve")

        bb_ctx.close()
        XF = ctx.enter_context(nc.sbuf_tensor("XF", [128, NDC * T], F32))

        def xf(dc):
            return XF[:, dc * T:(dc + 1) * T]

        new_block()

        xres_units = [None] * 8

        def load_xres(u):
            xres_units[u] = ring_load(lambda e, t, u=u: e.dma_start(
                out=t[:].rearrange("p (j t) -> p j t", j=2),
                in_=xt[u * 256:(u + 1) * 256, :].rearrange("(j p) t -> p j t", p=128)))

        load_xres(0)
        for ec in range(NDC):
            psi = ec % 4
            if ec % 2 == 0 and ec // 2 + 1 < 8:
                load_xres(ec // 2 + 1)
            w = mm_waits(psi, [("ve", o_wready), ("ve", y_done)])

            def omms(e, ec=ec, psi=psi):
                inst = None
                for dc in range(NDC):
                    lhsT = wr(dc)[:, ec * 128:(ec + 1) * 128]
                    st, sp = (dc == 0), (dc == NDC - 1)
                    e.matmul(PS[psi][:, 0:512], lhsT, b1(dc)[:, 0:512],
                             start=st, stop=sp)
                    inst = e.matmul(PS[psi][:, 512:1024], lhsT,
                                    b1(dc)[:, 512:1024], start=st, stop=sp)
                return inst
            op("tensor", omms, inc=("te", 1), waits=w)
            si = xres_units[ec // 2]
            j = ec % 2
            op("vector", lambda e, ec=ec, psi=psi, si=si, j=j: e.tensor_add(
                xf(ec), PS[psi][:], STG[si][:, j * 1024:(j + 1) * 1024]),
                inc=("ve", 1), waits=[("te", cnt("te")), ring_wait(si)])
            ps_free[psi] = ("ve", cnt("ve"))
            if j == 1:
                ring_done(si, "ve")
        x1_done = cnt("ve")

        wr_ctx.close()
        xn_ctx = ExitStack()
        XN = xn_ctx.enter_context(
            nc.sbuf_tensor("XN", [128, NDC * T], BF16, side="right"))
        SCI = xn_ctx.enter_context(
            nc.sbuf_tensor("SCI", [128, T], I32, side="right"))

        def xnt(dc):
            return XN[:, dc * T:(dc + 1) * T]

        new_block()

        def ln_stats(src_fn, src_done):
            for dc in range(NDC):
                op("vector", lambda e, dc=dc, src_fn=src_fn: e.tensor_copy(
                    SCB[:, 0:T], src_fn(dc)),
                    inc=("ve", 1), waits=[("ve", src_done), ("te", cnt("te"))])
                op("scalar", lambda e, dc=dc, src_fn=src_fn: e.activation(
                    SCB[:, T:2 * T], src_fn(dc), AF.Square),
                    inc=("se", 1), waits=[("ve", src_done), ("te", cnt("te"))])

                def smm(e, dc=dc):
                    st, sp = (dc == 0), (dc == NDC - 1)
                    e.matmul(PS[0][0:1, 0:512], ONEC[:], SCB[:, 0:512],
                             start=st, stop=sp)
                    e.matmul(PS[0][0:1, 512:1024], ONEC[:], SCB[:, 512:1024],
                             start=st, stop=sp)
                    e.matmul(PS[1][0:1, 0:512], ONEC[:], SCB[:, T:T + 512],
                             start=st, stop=sp)
                    return e.matmul(PS[1][0:1, 512:1024], ONEC[:],
                                    SCB[:, T + 512:2 * T], start=st, stop=sp)
                w = [("ve", cnt("ve")), ("se", cnt("se")), ("w0", 80)]
                if dc == 0:
                    for psi in (0, 1):
                        if ps_free[psi] is not None:
                            w.append(ps_free[psi])
                op("tensor", smm, inc=("te", 1), waits=w)

        def ln_rows(eps_ap):
            tec = cnt("te")
            op("vector", lambda e: e.tensor_single_scalar(
                R0, PS[0][0:1, :], INV_D, ALU.mult), inc=("ve", 1),
                waits=[("te", tec)])
            ps_free[0] = ("ve", cnt("ve"))
            op("vector", lambda e: e.tensor_mul(SCROW, R0, R0), inc=("ve", 1))
            op("vector", lambda e: e.scalar_tensor_tensor(
                R1, PS[1][0:1, :], INV_D, SCROW, ALU.mult, ALU.subtract),
                inc=("ve", 1), waits=[("te", tec)])
            ps_free[1] = ("ve", cnt("ve"))
            op("scalar", lambda e, eps_ap=eps_ap: e.activation(
                R1, R1, AF.Sqrt, bias=eps_ap),
                inc=("se", 1), waits=[("ve", cnt("ve"))])
            op("vector", lambda e: e.reciprocal(SCROW, R1),
               inc=("ve", 1), waits=[("se", cnt("se"))])
            op("vector", lambda e: e.tensor_single_scalar(
                R0, R0, -1.0, ALU.mult), inc=("ve", 1))

        ln_stats(xf, x1_done)
        ln_rows(c_eps5)
        bcast(R0, 2, scra(1), ("ve", cnt("ve")))
        bcast(SCROW, 3, scra(2), ("ve", cnt("ve")))
        bc1 = cnt("ve")
        for dc in range(NDC):
            op("vector", lambda e, dc=dc: e.tensor_add(xf(dc), xf(dc), scra(1)),
               inc=("ve", 1), waits=[("ve", bc1)])
            op("vector", lambda e, dc=dc: e.tensor_mul(xf(dc), xf(dc), scra(2)),
               inc=("ve", 1))
        x2_done = cnt("ve")

        for j in range(2):
            op("gpsimd", lambda e, j=j: e.dma_start(
                out=x2sp[j * 1024:(j + 1) * 1024, :].rearrange(
                    "(k p) t -> p k t", p=128),
                in_=XF[:, j * 8 * T:(j + 1) * 8 * T].rearrange(
                    "p (k t) -> p k t", k=8)),
                inc=("gd", 16), waits=[("ve", x2_done)])
        x2_spill = cnt("gd")

        ln_stats(xf, x2_done)
        ln_rows(c_eps8)
        bcast(R0, 2, scra(1), ("ve", cnt("ve")))
        bcast(SCROW, 3, scra(2), ("ve", cnt("ve")))
        bc2 = cnt("ve")
        for dc in range(NDC):
            op("vector", lambda e, dc=dc: e.tensor_add(scra(0), xf(dc), scra(1)),
               inc=("ve", 1), waits=[("ve", bc2)])
            op("vector", lambda e, dc=dc: e.tensor_mul(xnt(dc), scra(0), scra(2)),
               inc=("ve", 1))
        xn_done = cnt("ve")

        for dc in range(NDC):
            op("scalar", lambda e, dc=dc: e.activation(
                SCB[:, 0:T], xnt(dc), AF.Abs),
                inc=("se", 1), waits=[("ve", cnt("ve")), ("ve", xn_done)])
            if dc == 0:
                op("vector", lambda e: e.tensor_copy(scra(0), SCB[:, 0:T]),
                   inc=("ve", 1), waits=[("se", cnt("se"))])
            else:
                op("vector", lambda e: e.tensor_max(scra(0), scra(0), SCB[:, 0:T]),
                   inc=("ve", 1), waits=[("se", cnt("se"))])
        tree_done = cnt("ve")
        for j in range(8):
            psi = j % 4
            op("tensor", lambda e, j=j, psi=psi: e.transpose(
                PS[psi][:, 0:128], scra(0)[:, j * 128:(j + 1) * 128], IDENT[:]),
                inc=("te", 1), waits=mm_waits(psi, [("ve", tree_done), ("w0", 80)]))
            op("vector", lambda e, j=j, psi=psi: e.reduce_max(
                scra(1)[:, j:j + 1], PS[psi][:, 0:128], axis=AX.X),
                inc=("ve", 1), waits=[("te", cnt("te"))])
            ps_free[psi] = ("ve", cnt("ve"))
        mc_done = cnt("ve")
        for j in range(8):
            psi = j % 4
            op("tensor", lambda e, j=j, psi=psi: e.transpose(
                PS[psi][0:1, 0:128], scra(1)[:, j:j + 1], IDENT[:]),
                inc=("te", 1), waits=mm_waits(psi, [("ve", mc_done)]))
            op("vector", lambda e, j=j, psi=psi: e.tensor_copy(
                R0[:, j * 128:(j + 1) * 128], PS[psi][0:1, 0:128]),
                inc=("ve", 1), waits=[("te", cnt("te"))])
            ps_free[psi] = ("ve", cnt("ve"))
        op("vector", lambda e: e.reciprocal(R1, R0), inc=("ve", 1))
        op("vector", lambda e: e.tensor_single_scalar(
            R1, R1, 127.0, ALU.mult), inc=("ve", 1))
        op("vector", lambda e: e.tensor_single_scalar(
            R0, R0, INV127, ALU.mult), inc=("ve", 1))
        bcast(R1, 2, scra(1), ("ve", cnt("ve")))
        bcast(R0, 3, scra(2), ("ve", cnt("ve")))
        sb_done = cnt("ve")

        for dc in range(NDC):
            op("vector", lambda e, dc=dc: e.tensor_mul(PS[0][:], xnt(dc), scra(1)),
               inc=("ve", 1),
               waits=mm_waits(0, [("ve", sb_done)] if dc == 0 else []))
            op("vector", lambda e: e.tensor_copy(SCI[:], PS[0][:]), inc=("ve", 1))
            op("scalar", lambda e: e.copy(scra(0), SCI[:]),
               inc=("se", 1), waits=[("ve", cnt("ve"))])
            op("vector", lambda e: e.tensor_sub(PS[1][:], PS[0][:], scra(0)),
               inc=("ve", 1),
               waits=[("se", cnt("se"))] + (mm_waits(1, []) if dc == 0 else []))
            op("vector", lambda e: e.scalar_tensor_tensor(
                scra(0), PS[1][:], 0.5, scra(0), ALU.is_ge, ALU.add),
                inc=("ve", 1))
            op("vector", lambda e: e.scalar_tensor_tensor(
                PS[0][:], PS[1][:], -0.5, scra(0), ALU.is_le, ALU.subtract),
                inc=("ve", 1))
            ps_free[1] = ("ve", cnt("ve"))
            op("vector", lambda e: e.tensor_scalar(
                PS[1][:], PS[0][:], -1.0, 127.0, ALU.mult, ALU.min), inc=("ve", 1))
            ps_free[0] = ("ve", cnt("ve"))
            op("vector", lambda e, dc=dc: e.scalar_tensor_tensor(
                b1(dc), PS[1][:], -128.0, scra(2), ALU.max, ALU.mult),
                inc=("ve", 1))
        q_done = cnt("ve")

        xn_ctx.close()
        ffn_ctx = ExitStack()
        SGN = ffn_ctx.enter_context(
            nc.sbuf_tensor("SGN", [128, 3 * 8192], BF16, side="right"))
        PB = ffn_ctx.enter_context(
            nc.sbuf_tensor("PB", [128, NHH * T], BF16, side="right"))

        def pb(hm):
            return PB[:, hm * T:(hm + 1) * T]

        def sgn_blk(s):
            return SGN[:, s * 8192:(s + 1) * 8192]

        sgn_war = [None, None, None]
        acc_started = False
        gbi = 0

        for hh in range(NHH):
            new_block()
            for hb in range(2):
                ready = {}
                for wi, whbm in ((0, wgq), (1, wuq)):
                    slot = gbi % 3
                    gbi += 1
                    for u in range(4):
                        si = ring_load(lambda e, t, hh=hh, hb=hb, u=u, whbm=whbm:
                                       e.dma_start(
                            out=t[:].rearrange("p (j h) -> p j h", j=4),
                            in_=whbm[hh, u * 512:(u + 1) * 512,
                                     hb * 512:(hb + 1) * 512].rearrange(
                                         "(j p) h -> p j h", p=128)))
                        w = [ring_wait(si)]
                        if sgn_war[slot] is not None:
                            w.append(sgn_war[slot])
                        op("scalar", lambda e, si=si, slot=slot, u=u: e.sign(
                            sgn_blk(slot)[:, u * 2048:(u + 1) * 2048], STG[si][:]),
                            inc=("se", 1), waits=w)
                        ring_done(si, "se")
                    ready[wi] = (slot, cnt("se"))

                gslot, gready = ready[0]
                uslot, uready = ready[1]
                for hml in range(4):
                    hm = hb * 4 + hml
                    psg = 0 if (hm % 2 == 0) else 2
                    psu = 1 if (hm % 2 == 0) else 3

                    def gu_mms(e, hml=hml, gslot=gslot, uslot=uslot,
                               psg=psg, psu=psu):
                        inst = None
                        for srcslot, psi in ((gslot, psg), (uslot, psu)):
                            for dc in range(NDC):
                                lhsT = sgn_blk(srcslot)[
                                    :, dc * 512 + hml * 128:
                                    dc * 512 + (hml + 1) * 128]
                                st, sp = (dc == 0), (dc == NDC - 1)
                                e.matmul(PS[psi][:, 0:512], lhsT,
                                         b1(dc)[:, 0:512], start=st, stop=sp)
                                inst = e.matmul(PS[psi][:, 512:1024], lhsT,
                                                b1(dc)[:, 512:1024],
                                                start=st, stop=sp)
                        return inst
                    w = mm_waits(psg, mm_waits(
                        psu, [("se", gready), ("se", uready), ("ve", q_done)]))
                    op("tensor", gu_mms, inc=("te", 1), waits=w)
                    tec = cnt("te")
                    sgn_war[gslot] = ("te", tec)
                    sgn_war[uslot] = ("te", tec)
                    op("scalar", lambda e, psg=psg: e.activation(
                        scra(0), PS[psg][:], AF.Silu),
                        inc=("se", 1), waits=[("te", tec), ("ve", cnt("ve"))])
                    ps_free[psg] = ("se", cnt("se"))
                    op("vector", lambda e, hm=hm, psu=psu: e.tensor_mul(
                        pb(hm), scra(0), PS[psu][:]),
                        inc=("ve", 1), waits=[("se", cnt("se"))])
                    ps_free[psu] = ("ve", cnt("ve"))
            p_done = cnt("ve")
            gu_te_done = cnt("te")

            nxt = gbi % 3
            sl_slots = [(nxt + 1) % 3, (nxt + 2) % 3]
            slice_aps = []
            for k in range(8):
                si = ring_load(lambda e, t, hh=hh, k=k: e.dma_start(
                    out=t[:],
                    in_=wdt[hh * HH + k * 128: hh * HH + (k + 1) * 128, :]))
                s = sl_slots[k // 4]
                dst = sgn_blk(s)[:, (k % 4) * 2048:((k % 4) + 1) * 2048]
                slice_aps.append(dst)
                w = [ring_wait(si), ("te", gu_te_done)]
                op("scalar", lambda e, si=si, dst=dst: e.sign(dst, STG[si][:]),
                   inc=("se", 1), waits=w)
                ring_done(si, "se")
            for s in sl_slots:
                sgn_war[s] = None
            sdown_done = cnt("se")

            for dc in range(NDC):
                psi = dc % 4

                def dmms(e, psi=psi, dc=dc, saps=list(slice_aps)):
                    inst = None
                    for k in range(8):
                        lhsT = saps[k][:, dc * 128:(dc + 1) * 128]
                        st, sp = (k == 0), (k == 7)
                        e.matmul(PS[psi][:, 0:512], lhsT, pb(k)[:, 0:512],
                                 start=st, stop=sp)
                        inst = e.matmul(PS[psi][:, 512:1024], lhsT,
                                        pb(k)[:, 512:1024], start=st, stop=sp)
                    return inst
                op("tensor", dmms, inc=("te", 1),
                   waits=mm_waits(psi, [("se", sdown_done), ("ve", p_done)]))
                tec = cnt("te")
                if not acc_started:
                    op("vector", lambda e, dc=dc, psi=psi: e.tensor_copy(
                        xf(dc), PS[psi][:]),
                        inc=("ve", 1), waits=[("te", tec), ("gd", x2_spill)])
                else:
                    op("vector", lambda e, dc=dc, psi=psi: e.tensor_add(
                        xf(dc), xf(dc), PS[psi][:]),
                        inc=("ve", 1), waits=[("te", tec)])
                ps_free[psi] = ("ve", cnt("ve"))
            acc_started = True
            for s in sl_slots:
                sgn_war[s] = ("te", cnt("te"))
        acc_done = cnt("ve")

        ffn_ctx.close()
        e_ctx = ExitStack()
        OST = [e_ctx.enter_context(
            nc.sbuf_tensor(f"OST{i}", [128, 2048], F32, side="right"))
            for i in range(3)]
        NMRC = e_ctx.enter_context(
            nc.sbuf_tensor("NMRC", [128, 8], F32, side="right"))
        RSTC = e_ctx.enter_context(
            nc.sbuf_tensor("RSTC", [128, 8], F32, side="right"))

        new_block()

        for u in range(8):
            si = ring_load(lambda e, t, u=u: e.dma_start(
                out=t[:].rearrange("p (j t) -> p j t", j=2),
                in_=x2sp[u * 256:(u + 1) * 256, :].rearrange(
                    "(j p) t -> p j t", p=128)),
                extra_waits=[("gd", x2_spill)] if u == 0 else ())
            for j in range(2):
                dc = 2 * u + j
                op("vector", lambda e, si=si, j=j, dc=dc: e.tensor_add(
                    xf(dc), xf(dc), STG[si][:, j * 1024:(j + 1) * 1024]),
                    inc=("ve", 1), waits=[ring_wait(si), ("ve", acc_done)])
            ring_done(si, "ve")
        v_done = cnt("ve")

        ln_stats(xf, v_done)
        ln_rows(c_eps5)
        op("vector", lambda e: e.tensor_mul(R0, R0, SCROW), inc=("ve", 1))
        rows_done = cnt("ve")

        for tb in range(8):
            psi = tb % 4

            def rtr(e, tb=tb, psi=psi):
                e.transpose(PS[psi][:, 0:1], R0[:, tb * 128:(tb + 1) * 128],
                            IDENT[0:1, 0:1])
                return e.transpose(PS[psi][:, 512:513],
                                   SCROW[:, tb * 128:(tb + 1) * 128],
                                   IDENT[0:1, 0:1])
            op("tensor", rtr, inc=("te", 1),
               waits=mm_waits(psi, [("ve", rows_done)]))
            op("vector", lambda e, tb=tb, psi=psi: e.tensor_copy(
                NMRC[:, tb:tb + 1], PS[psi][:, 0:1]),
                inc=("ve", 1), waits=[("te", cnt("te"))])
            op("vector", lambda e, tb=tb, psi=psi: e.tensor_copy(
                RSTC[:, tb:tb + 1], PS[psi][:, 512:513]), inc=("ve", 1))
            ps_free[psi] = ("ve", cnt("ve"))
        cols_done = cnt("ve")

        ofill = [0, 0, 0]
        for tb in range(8):
            slot = tb % 3
            for dc in range(NDC):
                psi = dc % 4
                op("tensor", lambda e, tb=tb, dc=dc, psi=psi: e.transpose(
                    PS[psi][:, 0:128], xf(dc)[:, tb * 128:(tb + 1) * 128],
                    IDENT[:]), inc=("te", 1),
                    waits=mm_waits(psi, [("ve", cols_done)]))
                w = [("te", cnt("te"))]
                if dc == 0 and ofill[slot] > 0:
                    w.append((f"o{slot}", 16 * ofill[slot]))
                op("scalar", lambda e, tb=tb, dc=dc, psi=psi, slot=slot:
                   e.activation(OST[slot][:, dc * 128:(dc + 1) * 128],
                                PS[psi][:, 0:128], AF.Identity,
                                bias=NMRC[:, tb:tb + 1],
                                scale=RSTC[:, tb:tb + 1]),
                   inc=("se", 1), waits=w)
                ps_free[psi] = ("se", cnt("se"))
            op("gpsimd", lambda e, tb=tb, slot=slot: e.dma_start(
                out=out[tb * 128:(tb + 1) * 128, :], in_=OST[slot][:]),
                inc=(f"o{slot}", 16), waits=[("se", cnt("se"))])
            ofill[slot] += 1

        for slot in range(3):
            op("gpsimd", lambda e: None,
               waits=[(f"o{slot}", 16 * ofill[slot])])

        for recs in blocks:
            with nc.Block() as blk:
                def make_body(engname, recs=recs):
                    def body(eng):
                        last_wait = {}
                        for (ename, waits, fn, inc) in recs:
                            if ename != engname:
                                continue
                            for (s, v) in waits:
                                if last_wait.get(s, 0) < v:
                                    eng.wait_ge(sems[s], v)
                                    last_wait[s] = v
                            inst = fn(eng)
                            if inc is not None and inst is not None:
                                inst.then_inc(sems[inc[0]], inc[1])
                    return body
                blk.sync(make_body("sync"))
                blk.tensor(make_body("tensor"))
                blk.vector(make_body("vector"))
                blk.scalar(make_body("scalar"))
                blk.gpsimd(make_body("gpsimd"))

        e_ctx.close()
    return nc


_NC_CACHE = {}


def _get_nc():
    if "nc" not in _NC_CACHE:
        _NC_CACHE["nc"] = build_nc()
    return _NC_CACHE["nc"]


def kernel(L0_attn_x, L0_attn_Wi, L0_attn_Wf, L0_attn_Wg, L0_attn_w_norm,
           L0_attn_Wo, L0_ffn_W_g, L0_ffn_W_u, L0_ffn_W_d):
    x = np.asarray(L0_attn_x, dtype=np.float32).reshape(8192, D)
    xT = np.ascontiguousarray(x.T)
    wft = np.ascontiguousarray(np.asarray(L0_attn_Wf, np.float32).T)
    wit = np.ascontiguousarray(np.asarray(L0_attn_Wi, np.float32).T)
    wgt = np.ascontiguousarray(np.asarray(L0_attn_Wg, np.float32).T)
    wot = np.ascontiguousarray(np.asarray(L0_attn_Wo, np.float32).T)
    wgq = np.ascontiguousarray(
        np.asarray(L0_ffn_W_g, np.float32).T.reshape(D, NHH, HH).transpose(1, 0, 2))
    wuq = np.ascontiguousarray(
        np.asarray(L0_ffn_W_u, np.float32).T.reshape(D, NHH, HH).transpose(1, 0, 2))
    wdt = np.ascontiguousarray(np.asarray(L0_ffn_W_d, np.float32).T)
    wn = np.ascontiguousarray(
        np.asarray(L0_attn_w_norm, np.float32).reshape(NDC, 128).T)
    onec = np.ones((128, 1), dtype=ml_dtypes.bfloat16)
    oner = np.ones((1, 128), dtype=np.float32)
    ident = np.eye(128, dtype=np.float32)
    consts = np.tile(np.array([[INV_D, 1e-5, 1e-8, 0.0]], np.float32), (128, 1))

    shared = dict(wft=wft, wit=wit, wgt=wgt, wot=wot, wgq=wgq, wuq=wuq,
                  wdt=wdt, wn=wn, onec=onec, oner=oner, ident=ident,
                  consts=consts)
    in_maps = []
    for c in range(NCORES):
        m = dict(shared)
        m["xt"] = np.ascontiguousarray(xT[:, c * T:(c + 1) * T])
        if c == 0:
            m["xw"] = np.zeros((D, 128), dtype=np.float32)
        else:
            m["xw"] = np.ascontiguousarray(xT[:, c * T - 128:c * T])
        in_maps.append(m)

    nc = _get_nc()
    res = run_bass_kernel_spmd(nc, in_maps, core_ids=list(range(NCORES)))
    outs = [np.asarray(r["out"], dtype=np.float32) for r in res.results]
    return np.concatenate(outs, axis=0).reshape(1, 8192, D)


# version 09
# speedup vs baseline: 1.1410x; 1.1410x over previous
import numpy as np
import ml_dtypes
from contextlib import ExitStack

import concourse.bass as bass
import concourse.mybir as mybir
from concourse.bass_utils import run_bass_kernel_spmd

F32 = mybir.dt.float32
BF16 = mybir.dt.bfloat16
I32 = mybir.dt.int32
AF = mybir.ActivationFunctionType
ALU = mybir.AluOpType
AX = mybir.AxisListType

D = 2048
T = 1024
H = 8192
NDC = D // 128
NHH = 8
HH = H // NHH
NCORES = 8

INV_D = 1.0 / D
INV127 = 1.0 / 127.0


def build_nc():
    nc = bass.Bass("TRN2", target_bir_lowering=False)

    xt = nc.dram_tensor("xt", [D, T], F32, kind="ExternalInput").ap()
    xw_in = nc.dram_tensor("xw", [D, 128], F32, kind="ExternalInput").ap()
    wft = nc.dram_tensor("wft", [D, D], F32, kind="ExternalInput").ap()
    wit = nc.dram_tensor("wit", [D, D], F32, kind="ExternalInput").ap()
    wgt = nc.dram_tensor("wgt", [D, D], F32, kind="ExternalInput").ap()
    wot = nc.dram_tensor("wot", [D, D], F32, kind="ExternalInput").ap()
    wgq = nc.dram_tensor("wgq", [NHH, D, HH], F32, kind="ExternalInput").ap()
    wuq = nc.dram_tensor("wuq", [NHH, D, HH], F32, kind="ExternalInput").ap()
    wdt = nc.dram_tensor("wdt", [H, D], F32, kind="ExternalInput").ap()
    wn_in = nc.dram_tensor("wn", [128, NDC], F32, kind="ExternalInput").ap()
    onec_in = nc.dram_tensor("onec", [128, 1], BF16, kind="ExternalInput").ap()
    oner_in = nc.dram_tensor("oner", [1, 128], F32, kind="ExternalInput").ap()
    ident_in = nc.dram_tensor("ident", [128, 128], F32, kind="ExternalInput").ap()
    consts_in = nc.dram_tensor("consts", [128, 4], F32, kind="ExternalInput").ap()
    x2sp = nc.dram_tensor("x2sp", [D, T], F32, kind="Internal").ap()
    out = nc.dram_tensor("out", [T, D], F32, kind="ExternalOutput").ap()

    sem_counts = {}
    blocks = []
    cur = []

    def new_block():
        nonlocal cur
        cur = []
        blocks.append(cur)

    def op(eng, fn, inc=None, waits=()):
        waits = [(s, v) for (s, v) in waits if v > 0]
        cur.append((eng, waits, fn, inc))
        if inc is not None:
            sem_counts[inc[0]] = sem_counts.get(inc[0], 0) + inc[1]

    def cnt(sem):
        return sem_counts.get(sem, 0)

    ctx = ExitStack()
    with ctx:
        semnames = ["w0", "w1", "w2", "gd", "o0", "o1", "o2", "te", "ve", "se"]
        sems = {n: ctx.enter_context(nc.semaphore(n)) for n in semnames}
        for n in semnames:
            sem_counts[n] = 0

        B1 = ctx.enter_context(nc.sbuf_tensor("B1", [128, NDC * T], BF16))
        STG = [ctx.enter_context(nc.sbuf_tensor(f"STG{i}", [128, 2048], F32))
               for i in range(3)]
        ROWS = ctx.enter_context(nc.sbuf_tensor("ROWS", [1, T], F32))
        SCRA = ctx.enter_context(nc.sbuf_tensor("SCRA", [128, 2 * T], F32))
        SCB = ctx.enter_context(nc.sbuf_tensor("SCB", [128, 2 * T], BF16))
        WNS = ctx.enter_context(nc.sbuf_tensor("WNS", [128, NDC], F32))
        ONEC = ctx.enter_context(nc.sbuf_tensor("ONEC", [128, 1], BF16))
        ONER = ctx.enter_context(nc.sbuf_tensor("ONER", [1, 128], F32))
        IDENT = ctx.enter_context(nc.sbuf_tensor("IDENT", [128, 128], F32))
        CONSTS = ctx.enter_context(nc.sbuf_tensor("CONSTS", [128, 4], F32))
        PS = [ctx.enter_context(nc.psum_tensor(f"PS{i}", [128, 1024], F32))
              for i in range(4)]

        def b1(dc):
            return B1[:, dc * T:(dc + 1) * T]

        R0 = ROWS[:, 0:T]
        SCROW = SCRA[0:1, 0:T]
        R1V = SCRA[0:1, T:2 * T]

        def scra(i):
            return SCRA[:, i * T:(i + 1) * T]

        c_eps5 = CONSTS[0:1, 1:2]
        c_eps8 = CONSTS[0:1, 2:3]

        ring = {"idx": 0, "fill": [0, 0, 0], "war": [None, None, None]}

        def ring_load(dma_fn, extra_waits=()):
            i = ring["idx"] % 3
            ring["idx"] += 1
            waits = list(extra_waits)
            if ring["war"][i] is not None:
                waits.append(ring["war"][i])
            op("sync", (lambda e, i=i: dma_fn(e, STG[i])),
               inc=(f"w{i}", 16), waits=waits)
            ring["fill"][i] += 1
            return i

        def ring_wait(i):
            return (f"w{i}", 16 * ring["fill"][i])

        def ring_done(i, eng):
            ring["war"][i] = (eng, cnt(eng))

        ps_free = [None, None, None, None]

        def mm_waits(psi, extra):
            w = list(extra)
            if ps_free[psi] is not None:
                w.append(ps_free[psi])
            return w

        def bcast(row_ap, psi, ready):
            def bmm(e, row_ap=row_ap, psi=psi):
                e.matmul(PS[psi][:, 0:512], ONER[:], row_ap[:, 0:512],
                         start=True, stop=True)
                return e.matmul(PS[psi][:, 512:1024], ONER[:],
                                row_ap[:, 512:1024], start=True, stop=True)
            op("tensor", bmm, inc=("te", 1),
               waits=mm_waits(psi, [ready, ("w0", 80)]))
            return cnt("te")

        wr_ctx = ExitStack()
        WR = wr_ctx.enter_context(
            nc.sbuf_tensor("WR", [128, NDC * 2048], BF16, side="right"))
        bb_ctx = ExitStack()
        B2 = bb_ctx.enter_context(
            nc.sbuf_tensor("B2", [128, NDC * T], BF16, side="right"))
        B3 = bb_ctx.enter_context(
            nc.sbuf_tensor("B3", [128, NDC * T], BF16, side="right"))
        FWB = bb_ctx.enter_context(
            nc.sbuf_tensor("FWB", [128, NDC * 128], BF16, side="right"))
        HINC = bb_ctx.enter_context(
            nc.sbuf_tensor("HINC", [128, NDC], F32, side="right"))

        def b2(dc):
            return B2[:, dc * T:(dc + 1) * T]

        def b3(dc):
            return B3[:, dc * T:(dc + 1) * T]

        def wr(dc):
            return WR[:, dc * 2048:(dc + 1) * 2048]

        new_block()

        op("sync", lambda e: e.dma_start(out=WNS[:], in_=wn_in[:, :]), inc=("w0", 16))
        op("sync", lambda e: e.dma_start(out=ONEC[:], in_=onec_in[:, :]), inc=("w0", 16))
        op("sync", lambda e: e.dma_start(out=ONER[:], in_=oner_in[:, :]), inc=("w0", 16))
        op("sync", lambda e: e.dma_start(out=IDENT[:], in_=ident_in[:, :]), inc=("w0", 16))
        op("sync", lambda e: e.dma_start(out=CONSTS[:], in_=consts_in[:, :]), inc=("w0", 16))
        ring["fill"][0] += 5

        for u in range(8):
            si = ring_load(lambda e, t, u=u: e.dma_start(
                out=t[:].rearrange("p (j t) -> p j t", j=2),
                in_=xt[u * 256:(u + 1) * 256, :].rearrange("(j p) t -> p j t", p=128)))
            for j in range(2):
                dc = 2 * u + j
                op("vector", lambda e, si=si, j=j, dc=dc: e.tensor_copy(
                    b1(dc), STG[si][:, j * 1024:(j + 1) * 1024]),
                    inc=("ve", 1), waits=[ring_wait(si)])
            ring_done(si, "ve")
        si = ring_load(lambda e, t: e.dma_start(
            out=t[:].rearrange("p (j w) -> p j w", j=16),
            in_=xw_in[:, :].rearrange("(j p) w -> p j w", p=128)))
        op("vector", lambda e, si=si: e.tensor_copy(SCB[:], STG[si][:]),
           inc=("ve", 1), waits=[ring_wait(si)])
        ring_done(si, "ve")
        x_cast_done = cnt("ve")

        def xw_bf(dc):
            return SCB[:, dc * 128:(dc + 1) * 128]

        def window_proj(evac_fn, wready):
            for ec in range(NDC):
                psi = ec % 4
                w = mm_waits(psi, [("ve", x_cast_done), ("ve", wready)])

                def wmms(e, ec=ec, psi=psi):
                    inst = None
                    for dc in range(NDC):
                        lhsT = wr(dc)[:, ec * 128:(ec + 1) * 128]
                        inst = e.matmul(PS[psi][:, 0:128], lhsT, xw_bf(dc),
                                        start=(dc == 0), stop=(dc == NDC - 1))
                    return inst
                op("tensor", wmms, inc=("te", 1), waits=w)
                evac_fn(ec, psi, cnt("te"))

        def stream_attn_weight(whbm):
            wr_last_use = cnt("te")
            for dc in range(NDC):
                si = ring_load(lambda e, t, dc=dc, whbm=whbm: e.dma_start(
                    out=t[:], in_=whbm[dc * 128:(dc + 1) * 128, :]))
                op("vector", lambda e, si=si, dc=dc: e.tensor_copy(wr(dc), STG[si][:]),
                   inc=("ve", 1), waits=[ring_wait(si), ("te", wr_last_use)])
                ring_done(si, "ve")
            return cnt("ve")

        def attn_proj(mov_fn, evac_fn, wready, mov_ready, slots=(0, 1, 2, 3)):
            for ec in range(NDC):
                psi = slots[ec % len(slots)]
                w = mm_waits(psi, [("ve", wready), ("ve", mov_ready)])

                def mms(e, ec=ec, psi=psi, mov_fn=mov_fn):
                    inst = None
                    for dc in range(NDC):
                        lhsT = wr(dc)[:, ec * 128:(ec + 1) * 128]
                        st, sp = (dc == 0), (dc == NDC - 1)
                        e.matmul(PS[psi][:, 0:512], lhsT, mov_fn(dc)[:, 0:512],
                                 start=st, stop=sp)
                        inst = e.matmul(PS[psi][:, 512:1024], lhsT,
                                        mov_fn(dc)[:, 512:1024], start=st, stop=sp)
                    return inst
                op("tensor", mms, inc=("te", 1), waits=w)
                evac_fn(ec, psi, cnt("te"))

        f_wready = stream_attn_weight(wft)

        def fw(ec):
            return FWB[:, ec * 128:(ec + 1) * 128]

        def fw_evac(ec, psi, tec):
            op("scalar", lambda e, ec=ec, psi=psi: e.copy(fw(ec), PS[psi][:, 0:128]),
               inc=("se", 1), waits=[("te", tec)])
            ps_free[psi] = ("se", cnt("se"))
        window_proj(fw_evac, f_wready)

        def f_evac(ec, psi, tec):
            op("scalar", lambda e, ec=ec, psi=psi: e.copy(b2(ec), PS[psi][:]),
               inc=("se", 1), waits=[("te", tec)])
            ps_free[psi] = ("se", cnt("se"))
        attn_proj(b1, f_evac, f_wready, x_cast_done)

        i_wready = stream_attn_weight(wit)

        def iw_evac(ec, psi, tec):
            s0 = scra(0)
            op("scalar", lambda e, psi=psi, s0=s0: e.activation(
                s0[:, 0:128], PS[psi][:, 0:128], AF.Silu),
                inc=("se", 1), waits=[("te", tec), ("ve", cnt("ve"))])
            ps_free[psi] = ("se", cnt("se"))
            op("scalar", lambda e, ec=ec, s0=s0: e.activation(
                s0[:, 128:256], fw(ec), AF.Sigmoid, scale=-1.0), inc=("se", 1))
            op("scalar", lambda e, ec=ec, s0=s0: e.activation(
                s0[:, 384:512], fw(ec), AF.Sigmoid), inc=("se", 1))
            op("vector", lambda e, s0=s0: e.tensor_mul(
                s0[:, 256:384], s0[:, 0:128], s0[:, 128:256]),
                inc=("ve", 1), waits=[("se", cnt("se"))])
            op("vector", lambda e, s0=s0: e.tensor_tensor_scan(
                s0[:, 512:640], s0[:, 384:512], s0[:, 256:384], 0.0,
                ALU.mult, ALU.add), inc=("ve", 1))
            op("scalar", lambda e, ec=ec, s0=s0: e.copy(
                HINC[:, ec:ec + 1], s0[:, 639:640]),
                inc=("se", 1), waits=[("ve", cnt("ve"))])
        window_proj(iw_evac, i_wready)

        def i_evac(ec, psi, tec):
            op("scalar", lambda e, psi=psi: e.activation(
                scra(0), PS[psi][:], AF.Silu),
                inc=("se", 1), waits=[("te", tec), ("ve", cnt("ve"))])
            ps_free[psi] = ("se", cnt("se"))
            op("scalar", lambda e, ec=ec: e.activation(
                scra(1), b2(ec), AF.Sigmoid, scale=-1.0), inc=("se", 1))
            op("vector", lambda e, ec=ec: e.tensor_mul(b3(ec), scra(0), scra(1)),
               inc=("ve", 1), waits=[("se", cnt("se"))])
            op("scalar", lambda e, ec=ec: e.activation(scra(0), b2(ec), AF.Sigmoid),
               inc=("se", 1), waits=[("ve", cnt("ve"))])
            op("vector", lambda e, ec=ec: e.tensor_copy(scra(1), b3(ec)),
               inc=("ve", 1))
            op("scalar", lambda e, ec=ec: e.activation(
                scra(1)[:, 0:1], scra(0)[:, 0:1], AF.Identity,
                bias=scra(1)[:, 0:1], scale=HINC[:, ec:ec + 1]),
                inc=("se", 1), waits=[("ve", cnt("ve"))])
            op("vector", lambda e, ec=ec: e.tensor_tensor_scan(
                b2(ec), scra(0), scra(1), 0.0, ALU.mult, ALU.add),
                inc=("ve", 1), waits=[("se", cnt("se"))])
        attn_proj(b1, i_evac, i_wready, x_cast_done)
        scan_done = cnt("ve")

        g_wready = stream_attn_weight(wgt)

        def g_evac(ec, psi, tec):
            op("scalar", lambda e, ec=ec, psi=psi: e.activation(
                b3(ec), PS[psi][:], AF.Silu),
                inc=("se", 1), waits=[("te", tec), ("ve", scan_done)])
            ps_free[psi] = ("se", cnt("se"))
            op("vector", lambda e, ec=ec: e.tensor_mul(b3(ec), b2(ec), b3(ec)),
               inc=("ve", 1), waits=[("se", cnt("se"))])
            op("scalar", lambda e, ec=ec: e.activation(
                SCB[:, 0:T], b3(ec), AF.Square), inc=("se", 1),
                waits=[("ve", cnt("ve")), ("te", cnt("te"))])

            def sqmm(e, ec=ec):
                e.matmul(PS[0][0:1, 0:512], ONEC[:], SCB[:, 0:512],
                         start=(ec == 0), stop=(ec == NDC - 1))
                return e.matmul(PS[0][0:1, 512:1024], ONEC[:], SCB[:, 512:1024],
                                start=(ec == 0), stop=(ec == NDC - 1))
            w = [("se", cnt("se")), ("w0", 80)]
            if ec == 0 and ps_free[0] is not None:
                w.append(ps_free[0])
            op("tensor", sqmm, inc=("te", 1), waits=w)
        attn_proj(b1, g_evac, g_wready, x_cast_done, slots=(1, 2, 3))
        g_mm_done = cnt("te")

        o_wready = stream_attn_weight(wot)
        op("vector", lambda e: e.tensor_single_scalar(
            R1V, PS[0][0:1, :], INV_D, ALU.mult),
            inc=("ve", 1), waits=[("te", cnt("te"))])
        ps_free[0] = ("ve", cnt("ve"))
        op("scalar", lambda e: e.activation(R1V, R1V, AF.Sqrt, bias=c_eps5),
           inc=("se", 1), waits=[("ve", cnt("ve")), ("w0", 80)])
        op("vector", lambda e: e.reciprocal(R0, R1V),
           inc=("ve", 1), waits=[("se", cnt("se"))])
        rb_te = bcast(R0, 1, ("ve", cnt("ve")))

        for dc in range(NDC):
            op("vector", lambda e, dc=dc: e.tensor_mul(scra(0), b3(dc), PS[1][:]),
               inc=("ve", 1), waits=[("te", rb_te), ("se", cnt("se")),
                                     ("te", g_mm_done)])
            op("vector", lambda e, dc=dc: e.tensor_single_scalar(
                b1(dc), scra(0), WNS[:, dc:dc + 1], ALU.mult), inc=("ve", 1))
        y_done = cnt("ve")
        ps_free[1] = ("ve", y_done)

        bb_ctx.close()
        XF = ctx.enter_context(nc.sbuf_tensor("XF", [128, NDC * T], F32))

        def xf(dc):
            return XF[:, dc * T:(dc + 1) * T]

        new_block()

        xres_units = [None] * 8

        def load_xres(u):
            xres_units[u] = ring_load(lambda e, t, u=u: e.dma_start(
                out=t[:].rearrange("p (j t) -> p j t", j=2),
                in_=xt[u * 256:(u + 1) * 256, :].rearrange("(j p) t -> p j t", p=128)))

        load_xres(0)
        for ec in range(NDC):
            psi = ec % 4
            if ec % 2 == 0 and ec // 2 + 1 < 8:
                load_xres(ec // 2 + 1)
            w = mm_waits(psi, [("ve", o_wready), ("ve", y_done)])

            def omms(e, ec=ec, psi=psi):
                inst = None
                for dc in range(NDC):
                    lhsT = wr(dc)[:, ec * 128:(ec + 1) * 128]
                    st, sp = (dc == 0), (dc == NDC - 1)
                    e.matmul(PS[psi][:, 0:512], lhsT, b1(dc)[:, 0:512],
                             start=st, stop=sp)
                    inst = e.matmul(PS[psi][:, 512:1024], lhsT,
                                    b1(dc)[:, 512:1024], start=st, stop=sp)
                return inst
            op("tensor", omms, inc=("te", 1), waits=w)
            si = xres_units[ec // 2]
            j = ec % 2
            op("vector", lambda e, ec=ec, psi=psi, si=si, j=j: e.tensor_add(
                xf(ec), PS[psi][:], STG[si][:, j * 1024:(j + 1) * 1024]),
                inc=("ve", 1), waits=[("te", cnt("te")), ring_wait(si)])
            ps_free[psi] = ("ve", cnt("ve"))
            if j == 1:
                ring_done(si, "ve")
        x1_done = cnt("ve")

        wr_ctx.close()
        xn_ctx = ExitStack()
        XN = xn_ctx.enter_context(
            nc.sbuf_tensor("XN", [128, NDC * T], BF16, side="right"))
        SCI = xn_ctx.enter_context(
            nc.sbuf_tensor("SCI", [128, T], I32, side="right"))

        def xnt(dc):
            return XN[:, dc * T:(dc + 1) * T]

        new_block()

        def ln_stats(src_fn, src_done):
            for dc in range(NDC):
                op("vector", lambda e, dc=dc, src_fn=src_fn: e.tensor_copy(
                    SCB[:, 0:T], src_fn(dc)),
                    inc=("ve", 1), waits=[("ve", src_done), ("te", cnt("te"))])
                op("scalar", lambda e, dc=dc, src_fn=src_fn: e.activation(
                    SCB[:, T:2 * T], src_fn(dc), AF.Square),
                    inc=("se", 1), waits=[("ve", src_done), ("te", cnt("te"))])

                def smm(e, dc=dc):
                    st, sp = (dc == 0), (dc == NDC - 1)
                    e.matmul(PS[0][0:1, 0:512], ONEC[:], SCB[:, 0:512],
                             start=st, stop=sp)
                    e.matmul(PS[0][0:1, 512:1024], ONEC[:], SCB[:, 512:1024],
                             start=st, stop=sp)
                    e.matmul(PS[1][0:1, 0:512], ONEC[:], SCB[:, T:T + 512],
                             start=st, stop=sp)
                    return e.matmul(PS[1][0:1, 512:1024], ONEC[:],
                                    SCB[:, T + 512:2 * T], start=st, stop=sp)
                w = [("ve", cnt("ve")), ("se", cnt("se")), ("w0", 80)]
                if dc == 0:
                    for psi in (0, 1):
                        if ps_free[psi] is not None:
                            w.append(ps_free[psi])
                op("tensor", smm, inc=("te", 1), waits=w)

        def ln_rows(eps_ap):
            tec = cnt("te")
            op("vector", lambda e: e.tensor_single_scalar(
                R0, PS[0][0:1, :], INV_D, ALU.mult), inc=("ve", 1),
                waits=[("te", tec)])
            ps_free[0] = ("ve", cnt("ve"))
            op("vector", lambda e: e.tensor_mul(SCROW, R0, R0), inc=("ve", 1))
            op("vector", lambda e: e.scalar_tensor_tensor(
                R1V, PS[1][0:1, :], INV_D, SCROW, ALU.mult, ALU.subtract),
                inc=("ve", 1), waits=[("te", tec)])
            ps_free[1] = ("ve", cnt("ve"))
            op("scalar", lambda e, eps_ap=eps_ap: e.activation(
                R1V, R1V, AF.Sqrt, bias=eps_ap),
                inc=("se", 1), waits=[("ve", cnt("ve"))])
            op("vector", lambda e: e.reciprocal(SCROW, R1V),
               inc=("ve", 1), waits=[("se", cnt("se"))])
            op("vector", lambda e: e.tensor_single_scalar(
                R0, R0, -1.0, ALU.mult), inc=("ve", 1))

        ln_stats(xf, x1_done)
        ln_rows(c_eps5)
        bcast(R0, 2, ("ve", cnt("ve")))
        bc1 = bcast(SCROW, 3, ("ve", cnt("ve")))
        for dc in range(NDC):
            op("vector", lambda e, dc=dc: e.tensor_add(xf(dc), xf(dc), PS[2][:]),
               inc=("ve", 1), waits=[("te", bc1)])
            op("vector", lambda e, dc=dc: e.tensor_mul(xf(dc), xf(dc), PS[3][:]),
               inc=("ve", 1))
        x2_done = cnt("ve")
        ps_free[2] = ("ve", x2_done)
        ps_free[3] = ("ve", x2_done)

        for j in range(2):
            op("gpsimd", lambda e, j=j: e.dma_start(
                out=x2sp[j * 1024:(j + 1) * 1024, :].rearrange(
                    "(k p) t -> p k t", p=128),
                in_=XF[:, j * 8 * T:(j + 1) * 8 * T].rearrange(
                    "p (k t) -> p k t", k=8)),
                inc=("gd", 16), waits=[("ve", x2_done)])
        x2_spill = cnt("gd")

        ln_stats(xf, x2_done)
        ln_rows(c_eps8)
        bcast(R0, 2, ("ve", cnt("ve")))
        bc2 = bcast(SCROW, 3, ("ve", cnt("ve")))
        for dc in range(NDC):
            op("vector", lambda e, dc=dc: e.tensor_add(scra(0), xf(dc), PS[2][:]),
               inc=("ve", 1), waits=[("te", bc2)])
            op("vector", lambda e, dc=dc: e.tensor_mul(xnt(dc), scra(0), PS[3][:]),
               inc=("ve", 1))
        xn_done = cnt("ve")
        ps_free[2] = ("ve", xn_done)
        ps_free[3] = ("ve", xn_done)

        for dc in range(NDC):
            op("scalar", lambda e, dc=dc: e.activation(
                SCB[:, 0:T], xnt(dc), AF.Abs),
                inc=("se", 1), waits=[("ve", cnt("ve")), ("ve", xn_done)])
            if dc == 0:
                op("vector", lambda e: e.tensor_copy(scra(0), SCB[:, 0:T]),
                   inc=("ve", 1), waits=[("se", cnt("se"))])
            else:
                op("vector", lambda e: e.tensor_max(scra(0), scra(0), SCB[:, 0:T]),
                   inc=("ve", 1), waits=[("se", cnt("se"))])
        tree_done = cnt("ve")
        for j in range(8):
            psi = j % 4
            op("tensor", lambda e, j=j, psi=psi: e.transpose(
                PS[psi][:, 0:128], scra(0)[:, j * 128:(j + 1) * 128], IDENT[:]),
                inc=("te", 1), waits=mm_waits(psi, [("ve", tree_done), ("w0", 80)]))
            op("vector", lambda e, j=j, psi=psi: e.reduce_max(
                scra(1)[:, j:j + 1], PS[psi][:, 0:128], axis=AX.X),
                inc=("ve", 1), waits=[("te", cnt("te"))])
            ps_free[psi] = ("ve", cnt("ve"))
        mc_done = cnt("ve")
        for j in range(8):
            psi = j % 4
            op("tensor", lambda e, j=j, psi=psi: e.transpose(
                PS[psi][0:1, 0:128], scra(1)[:, j:j + 1], IDENT[:]),
                inc=("te", 1), waits=mm_waits(psi, [("ve", mc_done)]))
            op("vector", lambda e, j=j, psi=psi: e.tensor_copy(
                R0[:, j * 128:(j + 1) * 128], PS[psi][0:1, 0:128]),
                inc=("ve", 1), waits=[("te", cnt("te"))])
            ps_free[psi] = ("ve", cnt("ve"))
        op("vector", lambda e: e.reciprocal(R1V, R0), inc=("ve", 1))
        op("vector", lambda e: e.tensor_single_scalar(
            R1V, R1V, 127.0, ALU.mult), inc=("ve", 1))
        op("vector", lambda e: e.tensor_single_scalar(
            R0, R0, INV127, ALU.mult), inc=("ve", 1))
        bcast(R1V, 2, ("ve", cnt("ve")))
        sb_te = bcast(R0, 3, ("ve", cnt("ve")))
        sb_done = cnt("ve")

        for dc in range(NDC):
            op("vector", lambda e, dc=dc: e.tensor_mul(PS[0][:], xnt(dc), PS[2][:]),
               inc=("ve", 1),
               waits=mm_waits(0, [("te", sb_te), ("ve", sb_done)] if dc == 0 else []))
            op("vector", lambda e: e.tensor_copy(SCI[:], PS[0][:]), inc=("ve", 1))
            op("scalar", lambda e: e.copy(scra(0), SCI[:]),
               inc=("se", 1), waits=[("ve", cnt("ve"))])
            op("vector", lambda e: e.tensor_sub(PS[1][:], PS[0][:], scra(0)),
               inc=("ve", 1),
               waits=[("se", cnt("se"))] + (mm_waits(1, []) if dc == 0 else []))
            op("vector", lambda e: e.scalar_tensor_tensor(
                scra(0), PS[1][:], 0.5, scra(0), ALU.is_ge, ALU.add),
                inc=("ve", 1))
            op("vector", lambda e: e.scalar_tensor_tensor(
                PS[0][:], PS[1][:], -0.5, scra(0), ALU.is_le, ALU.subtract),
                inc=("ve", 1))
            op("vector", lambda e: e.tensor_scalar(
                scra(0), PS[0][:], -1.0, 127.0, ALU.mult, ALU.min), inc=("ve", 1))
            op("vector", lambda e, dc=dc: e.scalar_tensor_tensor(
                b1(dc), scra(0), -128.0, PS[3][:], ALU.max, ALU.mult),
                inc=("ve", 1))
        q_done = cnt("ve")
        ps_free[0] = ("ve", q_done)
        ps_free[1] = ("ve", q_done)
        ps_free[2] = ("ve", q_done)
        ps_free[3] = ("ve", q_done)

        xn_ctx.close()
        ffn_ctx = ExitStack()
        SGN = ffn_ctx.enter_context(
            nc.sbuf_tensor("SGN", [128, 3 * 8192], BF16, side="right"))
        PB = ffn_ctx.enter_context(
            nc.sbuf_tensor("PB", [128, NHH * T], BF16, side="right"))

        def pb(hm):
            return PB[:, hm * T:(hm + 1) * T]

        def sgn_blk(s):
            return SGN[:, s * 8192:(s + 1) * 8192]

        sgn_war = [None, None, None]
        acc_started = False
        gbi = 0

        for hh in range(NHH):
            new_block()
            gbufs = [scra(0), scra(1), SCB[:, 0:T], SCB[:, T:2 * T]]
            hb0_u_te = None
            for hb in range(2):
                gslot = gbi % 3
                gbi += 1
                gsig = []
                for u in range(4):
                    si = ring_load(lambda e, t, hh=hh, hb=hb, u=u: e.dma_start(
                        out=t[:].rearrange("p (j h) -> p j h", j=4),
                        in_=wgq[hh, u * 512:(u + 1) * 512,
                                hb * 512:(hb + 1) * 512].rearrange(
                                    "(j p) h -> p j h", p=128)))
                    w = [ring_wait(si)]
                    if sgn_war[gslot] is not None:
                        w.append(sgn_war[gslot])
                    op("scalar", lambda e, si=si, gslot=gslot, u=u: e.sign(
                        sgn_blk(gslot)[:, u * 2048:(u + 1) * 2048], STG[si][:]),
                        inc=("se", 1), waits=w)
                    ring_done(si, "se")
                gready = cnt("se")
                uslot = gbi % 3
                gbi += 1
                usig_done = None
                for hml in range(4):
                    psg = 0 if (hml % 2 == 0) else 2

                    def g_chain(e, hml=hml, gslot=gslot, psg=psg):
                        inst = None
                        for dc in range(NDC):
                            lhsT = sgn_blk(gslot)[
                                :, dc * 512 + hml * 128: dc * 512 + (hml + 1) * 128]
                            st, sp = (dc == 0), (dc == NDC - 1)
                            e.matmul(PS[psg][:, 0:512], lhsT, b1(dc)[:, 0:512],
                                     start=st, stop=sp)
                            inst = e.matmul(PS[psg][:, 512:1024], lhsT,
                                            b1(dc)[:, 512:1024], start=st, stop=sp)
                        return inst
                    op("tensor", g_chain, inc=("te", 1),
                       waits=mm_waits(psg, [("se", gready), ("ve", q_done)]))
                    sgn_war[gslot] = ("te", cnt("te"))
                    op("scalar", lambda e, hml=hml, psg=psg: e.activation(
                        gbufs[hml], PS[psg][:], AF.Silu),
                        inc=("se", 1), waits=[("te", cnt("te")), ("ve", cnt("ve"))])
                    ps_free[psg] = ("se", cnt("se"))
                    si = ring_load(lambda e, t, hh=hh, hb=hb, u=hml: e.dma_start(
                        out=t[:].rearrange("p (j h) -> p j h", j=4),
                        in_=wuq[hh, u * 512:(u + 1) * 512,
                                hb * 512:(hb + 1) * 512].rearrange(
                                    "(j p) h -> p j h", p=128)))
                    w = [ring_wait(si)]
                    if sgn_war[uslot] is not None and hml == 0:
                        w.append(sgn_war[uslot])
                    op("scalar", lambda e, si=si, uslot=uslot, u=hml: e.sign(
                        sgn_blk(uslot)[:, u * 2048:(u + 1) * 2048], STG[si][:]),
                        inc=("se", 1), waits=w)
                    ring_done(si, "se")
                    usig_done = cnt("se")
                for hml in range(4):
                    hm = hb * 4 + hml
                    psu = 1 if (hml % 2 == 0) else 3

                    def u_chain(e, hml=hml, uslot=uslot, psu=psu):
                        inst = None
                        for dc in range(NDC):
                            lhsT = sgn_blk(uslot)[
                                :, dc * 512 + hml * 128: dc * 512 + (hml + 1) * 128]
                            st, sp = (dc == 0), (dc == NDC - 1)
                            e.matmul(PS[psu][:, 0:512], lhsT, b1(dc)[:, 0:512],
                                     start=st, stop=sp)
                            inst = e.matmul(PS[psu][:, 512:1024], lhsT,
                                            b1(dc)[:, 512:1024], start=st, stop=sp)
                        return inst
                    op("tensor", u_chain, inc=("te", 1),
                       waits=mm_waits(psu, [("se", usig_done)]))
                    sgn_war[uslot] = ("te", cnt("te"))
                    op("vector", lambda e, hm=hm, hml=hml, psu=psu: e.tensor_mul(
                        pb(hm), gbufs[hml], PS[psu][:]),
                        inc=("ve", 1), waits=[("te", cnt("te")), ("se", cnt("se"))])
                    ps_free[psu] = ("ve", cnt("ve"))
                if hb == 0:
                    hb0_u_te = cnt("te")
            p_done = cnt("ve")

            nxt = gbi % 3
            sl_slots = [(nxt + 1) % 3, (nxt + 2) % 3]
            slice_aps = []
            for k in range(8):
                si = ring_load(lambda e, t, hh=hh, k=k: e.dma_start(
                    out=t[:],
                    in_=wdt[hh * HH + k * 128: hh * HH + (k + 1) * 128, :]))
                sslot = sl_slots[k // 4]
                dst = sgn_blk(sslot)[:, (k % 4) * 2048:((k % 4) + 1) * 2048]
                slice_aps.append(dst)
                w = [ring_wait(si), ("te", hb0_u_te)]
                if sgn_war[sslot] is not None:
                    w.append(sgn_war[sslot])
                op("scalar", lambda e, si=si, dst=dst: e.sign(dst, STG[si][:]),
                   inc=("se", 1), waits=w)
                ring_done(si, "se")
            sdown_done = cnt("se")
            for sslot in sl_slots:
                sgn_war[sslot] = None

            for dc in range(NDC):
                psi = dc % 4

                def dmms(e, psi=psi, dc=dc, saps=list(slice_aps)):
                    inst = None
                    for k in range(8):
                        lhsT = saps[k][:, dc * 128:(dc + 1) * 128]
                        st, sp = (k == 0), (k == 7)
                        e.matmul(PS[psi][:, 0:512], lhsT, pb(k)[:, 0:512],
                                 start=st, stop=sp)
                        inst = e.matmul(PS[psi][:, 512:1024], lhsT,
                                        pb(k)[:, 512:1024], start=st, stop=sp)
                    return inst
                op("tensor", dmms, inc=("te", 1),
                   waits=mm_waits(psi, [("se", sdown_done), ("ve", p_done)]))
                tec = cnt("te")
                if not acc_started:
                    op("vector", lambda e, dc=dc, psi=psi: e.tensor_copy(
                        xf(dc), PS[psi][:]),
                        inc=("ve", 1), waits=[("te", tec), ("gd", x2_spill)])
                else:
                    op("vector", lambda e, dc=dc, psi=psi: e.tensor_add(
                        xf(dc), xf(dc), PS[psi][:]),
                        inc=("ve", 1), waits=[("te", tec)])
                ps_free[psi] = ("ve", cnt("ve"))
            acc_started = True
            down_te = cnt("te")
            for sslot in sl_slots:
                sgn_war[sslot] = ("te", down_te)
        acc_done = cnt("ve")

        ffn_ctx.close()
        e_ctx = ExitStack()
        OST = [e_ctx.enter_context(
            nc.sbuf_tensor(f"OST{i}", [128, 2048], F32, side="right"))
            for i in range(3)]
        NMRC = e_ctx.enter_context(
            nc.sbuf_tensor("NMRC", [128, 8], F32, side="right"))
        RSTC = e_ctx.enter_context(
            nc.sbuf_tensor("RSTC", [128, 8], F32, side="right"))

        new_block()

        for u in range(8):
            si = ring_load(lambda e, t, u=u: e.dma_start(
                out=t[:].rearrange("p (j t) -> p j t", j=2),
                in_=x2sp[u * 256:(u + 1) * 256, :].rearrange(
                    "(j p) t -> p j t", p=128)),
                extra_waits=[("gd", x2_spill)] if u == 0 else ())
            for j in range(2):
                dc = 2 * u + j
                op("vector", lambda e, si=si, j=j, dc=dc: e.tensor_add(
                    xf(dc), xf(dc), STG[si][:, j * 1024:(j + 1) * 1024]),
                    inc=("ve", 1), waits=[ring_wait(si), ("ve", acc_done)])
            ring_done(si, "ve")
        v_done = cnt("ve")

        ln_stats(xf, v_done)
        ln_rows(c_eps5)
        op("vector", lambda e: e.tensor_mul(R0, R0, SCROW), inc=("ve", 1))
        rows_done = cnt("ve")

        for tb in range(8):
            psi = tb % 4

            def rtr(e, tb=tb, psi=psi):
                e.transpose(PS[psi][:, 0:1], R0[:, tb * 128:(tb + 1) * 128],
                            IDENT[0:1, 0:1])
                return e.transpose(PS[psi][:, 512:513],
                                   SCROW[:, tb * 128:(tb + 1) * 128],
                                   IDENT[0:1, 0:1])
            op("tensor", rtr, inc=("te", 1),
               waits=mm_waits(psi, [("ve", rows_done)]))
            op("vector", lambda e, tb=tb, psi=psi: e.tensor_copy(
                NMRC[:, tb:tb + 1], PS[psi][:, 0:1]),
                inc=("ve", 1), waits=[("te", cnt("te"))])
            op("vector", lambda e, tb=tb, psi=psi: e.tensor_copy(
                RSTC[:, tb:tb + 1], PS[psi][:, 512:513]), inc=("ve", 1))
            ps_free[psi] = ("ve", cnt("ve"))
        cols_done = cnt("ve")

        ofill = [0, 0, 0]
        for tb in range(8):
            slot = tb % 3
            for dc in range(NDC):
                psi = dc % 4
                op("tensor", lambda e, tb=tb, dc=dc, psi=psi: e.transpose(
                    PS[psi][:, 0:128], xf(dc)[:, tb * 128:(tb + 1) * 128],
                    IDENT[:]), inc=("te", 1),
                    waits=mm_waits(psi, [("ve", cols_done)]))
                w = [("te", cnt("te"))]
                if dc == 0 and ofill[slot] > 0:
                    w.append((f"o{slot}", 16 * ofill[slot]))
                op("scalar", lambda e, tb=tb, dc=dc, psi=psi, slot=slot:
                   e.activation(OST[slot][:, dc * 128:(dc + 1) * 128],
                                PS[psi][:, 0:128], AF.Identity,
                                bias=NMRC[:, tb:tb + 1],
                                scale=RSTC[:, tb:tb + 1]),
                   inc=("se", 1), waits=w)
                ps_free[psi] = ("se", cnt("se"))
            op("gpsimd", lambda e, tb=tb, slot=slot: e.dma_start(
                out=out[tb * 128:(tb + 1) * 128, :], in_=OST[slot][:]),
                inc=(f"o{slot}", 16), waits=[("se", cnt("se"))])
            ofill[slot] += 1

        for slot in range(3):
            op("gpsimd", lambda e: None,
               waits=[(f"o{slot}", 16 * ofill[slot])])

        for recs in blocks:
            with nc.Block() as blk:
                def make_body(engname, recs=recs):
                    def body(eng):
                        last_wait = {}
                        for (ename, waits, fn, inc) in recs:
                            if ename != engname:
                                continue
                            for (s, v) in waits:
                                if last_wait.get(s, 0) < v:
                                    eng.wait_ge(sems[s], v)
                                    last_wait[s] = v
                            inst = fn(eng)
                            if inc is not None and inst is not None:
                                inst.then_inc(sems[inc[0]], inc[1])
                    return body
                blk.sync(make_body("sync"))
                blk.tensor(make_body("tensor"))
                blk.vector(make_body("vector"))
                blk.scalar(make_body("scalar"))
                blk.gpsimd(make_body("gpsimd"))

        e_ctx.close()
    return nc


_NC_CACHE = {}


def _get_nc():
    if "nc" not in _NC_CACHE:
        _NC_CACHE["nc"] = build_nc()
    return _NC_CACHE["nc"]


def kernel(L0_attn_x, L0_attn_Wi, L0_attn_Wf, L0_attn_Wg, L0_attn_w_norm,
           L0_attn_Wo, L0_ffn_W_g, L0_ffn_W_u, L0_ffn_W_d):
    x = np.asarray(L0_attn_x, dtype=np.float32).reshape(8192, D)
    xT = np.ascontiguousarray(x.T)
    wft = np.ascontiguousarray(np.asarray(L0_attn_Wf, np.float32).T)
    wit = np.ascontiguousarray(np.asarray(L0_attn_Wi, np.float32).T)
    wgt = np.ascontiguousarray(np.asarray(L0_attn_Wg, np.float32).T)
    wot = np.ascontiguousarray(np.asarray(L0_attn_Wo, np.float32).T)
    wgq = np.ascontiguousarray(
        np.asarray(L0_ffn_W_g, np.float32).T.reshape(D, NHH, HH).transpose(1, 0, 2))
    wuq = np.ascontiguousarray(
        np.asarray(L0_ffn_W_u, np.float32).T.reshape(D, NHH, HH).transpose(1, 0, 2))
    wdt = np.ascontiguousarray(np.asarray(L0_ffn_W_d, np.float32).T)
    wn = np.ascontiguousarray(
        np.asarray(L0_attn_w_norm, np.float32).reshape(NDC, 128).T)
    onec = np.ones((128, 1), dtype=ml_dtypes.bfloat16)
    oner = np.ones((1, 128), dtype=np.float32)
    ident = np.eye(128, dtype=np.float32)
    consts = np.tile(np.array([[INV_D, 1e-5, 1e-8, 0.0]], np.float32), (128, 1))

    shared = dict(wft=wft, wit=wit, wgt=wgt, wot=wot, wgq=wgq, wuq=wuq,
                  wdt=wdt, wn=wn, onec=onec, oner=oner, ident=ident,
                  consts=consts)
    in_maps = []
    for c in range(NCORES):
        m = dict(shared)
        m["xt"] = np.ascontiguousarray(xT[:, c * T:(c + 1) * T])
        if c == 0:
            m["xw"] = np.zeros((D, 128), dtype=np.float32)
        else:
            m["xw"] = np.ascontiguousarray(xT[:, c * T - 128:c * T])
        in_maps.append(m)

    nc = _get_nc()
    res = run_bass_kernel_spmd(nc, in_maps, core_ids=list(range(NCORES)))
    outs = [np.asarray(r["out"], dtype=np.float32) for r in res.results]
    return np.concatenate(outs, axis=0).reshape(1, 8192, D)


# version 10
# speedup vs baseline: 1.1896x; 1.0425x over previous
import numpy as np
import ml_dtypes
from contextlib import ExitStack

import concourse.bass as bass
import concourse.mybir as mybir
from concourse.bass_utils import run_bass_kernel_spmd

F32 = mybir.dt.float32
BF16 = mybir.dt.bfloat16
I32 = mybir.dt.int32
AF = mybir.ActivationFunctionType
ALU = mybir.AluOpType
AX = mybir.AxisListType

D = 2048
T = 1024
H = 8192
NDC = D // 128
NHH = 8
HH = H // NHH
NCORES = 8

INV_D = 1.0 / D
INV127 = 1.0 / 127.0


def build_nc():
    nc = bass.Bass("TRN2", target_bir_lowering=False)

    xt = nc.dram_tensor("xt", [D, T], F32, kind="ExternalInput").ap()
    xw_in = nc.dram_tensor("xw", [D, 128], F32, kind="ExternalInput").ap()
    wft = nc.dram_tensor("wft", [D, D], F32, kind="ExternalInput").ap()
    wit = nc.dram_tensor("wit", [D, D], F32, kind="ExternalInput").ap()
    wgt = nc.dram_tensor("wgt", [D, D], F32, kind="ExternalInput").ap()
    wot = nc.dram_tensor("wot", [D, D], F32, kind="ExternalInput").ap()
    wgq = nc.dram_tensor("wgq", [NHH, D, HH], F32, kind="ExternalInput").ap()
    wuq = nc.dram_tensor("wuq", [NHH, D, HH], F32, kind="ExternalInput").ap()
    wdt = nc.dram_tensor("wdt", [H, D], F32, kind="ExternalInput").ap()
    wn_in = nc.dram_tensor("wn", [128, NDC], F32, kind="ExternalInput").ap()
    onec_in = nc.dram_tensor("onec", [128, 1], BF16, kind="ExternalInput").ap()
    oner_in = nc.dram_tensor("oner", [1, 128], F32, kind="ExternalInput").ap()
    ident_in = nc.dram_tensor("ident", [128, 128], F32, kind="ExternalInput").ap()
    consts_in = nc.dram_tensor("consts", [128, 4], F32, kind="ExternalInput").ap()
    x2sp = nc.dram_tensor("x2sp", [D, T], F32, kind="Internal").ap()
    out = nc.dram_tensor("out", [T, D], F32, kind="ExternalOutput").ap()

    sem_counts = {}
    blocks = []
    cur = []

    def new_block():
        nonlocal cur
        cur = []
        blocks.append(cur)

    def op(eng, fn, inc=None, waits=()):
        waits = [(s, v) for (s, v) in waits if v > 0]
        cur.append((eng, waits, fn, inc))
        if inc is not None:
            sem_counts[inc[0]] = sem_counts.get(inc[0], 0) + inc[1]

    def cnt(sem):
        return sem_counts.get(sem, 0)

    ctx = ExitStack()
    with ctx:
        semnames = ["w0", "w1", "w2", "gd", "o0", "o1", "o2", "te", "ve", "se"]
        sems = {n: ctx.enter_context(nc.semaphore(n)) for n in semnames}
        for n in semnames:
            sem_counts[n] = 0

        B1 = ctx.enter_context(nc.sbuf_tensor("B1", [128, NDC * T], BF16))
        STG = [ctx.enter_context(nc.sbuf_tensor(f"STG{i}", [128, 2048], F32))
               for i in range(3)]
        ROWS = ctx.enter_context(nc.sbuf_tensor("ROWS", [1, T], F32))
        SCRA = ctx.enter_context(nc.sbuf_tensor("SCRA", [128, 2 * T], F32))
        SCB = ctx.enter_context(nc.sbuf_tensor("SCB", [128, 2 * T], BF16))
        WNS = ctx.enter_context(nc.sbuf_tensor("WNS", [128, NDC], F32))
        ONEC = ctx.enter_context(nc.sbuf_tensor("ONEC", [128, 1], BF16))
        ONER = ctx.enter_context(nc.sbuf_tensor("ONER", [1, 128], F32))
        IDENT = ctx.enter_context(nc.sbuf_tensor("IDENT", [128, 128], F32))
        CONSTS = ctx.enter_context(nc.sbuf_tensor("CONSTS", [128, 4], F32))
        PS = [ctx.enter_context(nc.psum_tensor(f"PS{i}", [128, 1024], F32))
              for i in range(4)]

        def b1(dc):
            return B1[:, dc * T:(dc + 1) * T]

        R0 = ROWS[:, 0:T]
        SCROW = SCRA[0:1, 0:T]
        R1V = SCRA[0:1, T:2 * T]

        def scra(i):
            return SCRA[:, i * T:(i + 1) * T]

        c_eps5 = CONSTS[0:1, 1:2]
        c_eps8 = CONSTS[0:1, 2:3]

        ring = {"idx": 0, "fill": [0, 0, 0], "war": [None, None, None]}

        def ring_load(dma_fn, extra_waits=()):
            i = ring["idx"] % 3
            ring["idx"] += 1
            waits = list(extra_waits)
            if ring["war"][i] is not None:
                waits.append(ring["war"][i])
            op("sync", (lambda e, i=i: dma_fn(e, STG[i])),
               inc=(f"w{i}", 16), waits=waits)
            ring["fill"][i] += 1
            return i

        def ring_wait(i):
            return (f"w{i}", 16 * ring["fill"][i])

        def ring_done(i, eng):
            ring["war"][i] = (eng, cnt(eng))

        ps_free = [None, None, None, None]

        def mm_waits(psi, extra):
            w = list(extra)
            if ps_free[psi] is not None:
                w.append(ps_free[psi])
            return w

        def bcast(row_ap, psi, ready, lo=0, hi=T):
            def bmm(e, row_ap=row_ap, psi=psi, lo=lo, hi=hi):
                inst = None
                for c0 in range(lo, hi, 512):
                    inst = e.matmul(PS[psi][:, c0:c0 + 512], ONER[:],
                                    row_ap[:, c0:c0 + 512], start=True, stop=True)
                return inst
            op("tensor", bmm, inc=("te", 1),
               waits=mm_waits(psi, [ready, ("w0", 80)]))
            return cnt("te")

        wr_ctx = ExitStack()
        WR = wr_ctx.enter_context(
            nc.sbuf_tensor("WR", [128, NDC * 2048], BF16, side="right"))
        bb_ctx = ExitStack()
        B2 = bb_ctx.enter_context(
            nc.sbuf_tensor("B2", [128, NDC * T], BF16, side="right"))
        B3 = bb_ctx.enter_context(
            nc.sbuf_tensor("B3", [128, NDC * T], BF16, side="right"))
        FWB = bb_ctx.enter_context(
            nc.sbuf_tensor("FWB", [128, NDC * 128], BF16, side="right"))
        HINC = bb_ctx.enter_context(
            nc.sbuf_tensor("HINC", [128, NDC], F32, side="right"))

        def b2(dc):
            return B2[:, dc * T:(dc + 1) * T]

        def b3(dc):
            return B3[:, dc * T:(dc + 1) * T]

        def wr(dc):
            return WR[:, dc * 2048:(dc + 1) * 2048]

        new_block()

        op("sync", lambda e: e.dma_start(out=WNS[:], in_=wn_in[:, :]), inc=("w0", 16))
        op("sync", lambda e: e.dma_start(out=ONEC[:], in_=onec_in[:, :]), inc=("w0", 16))
        op("sync", lambda e: e.dma_start(out=ONER[:], in_=oner_in[:, :]), inc=("w0", 16))
        op("sync", lambda e: e.dma_start(out=IDENT[:], in_=ident_in[:, :]), inc=("w0", 16))
        op("sync", lambda e: e.dma_start(out=CONSTS[:], in_=consts_in[:, :]), inc=("w0", 16))
        ring["fill"][0] += 5

        for u in range(8):
            si = ring_load(lambda e, t, u=u: e.dma_start(
                out=t[:].rearrange("p (j t) -> p j t", j=2),
                in_=xt[u * 256:(u + 1) * 256, :].rearrange("(j p) t -> p j t", p=128)))
            for j in range(2):
                dc = 2 * u + j
                op("vector", lambda e, si=si, j=j, dc=dc: e.tensor_copy(
                    b1(dc), STG[si][:, j * 1024:(j + 1) * 1024]),
                    inc=("ve", 1), waits=[ring_wait(si)])
            ring_done(si, "ve")
        si = ring_load(lambda e, t: e.dma_start(
            out=t[:].rearrange("p (j w) -> p j w", j=16),
            in_=xw_in[:, :].rearrange("(j p) w -> p j w", p=128)))
        op("vector", lambda e, si=si: e.tensor_copy(SCB[:], STG[si][:]),
           inc=("ve", 1), waits=[ring_wait(si)])
        ring_done(si, "ve")
        x_cast_done = cnt("ve")

        def xw_bf(dc):
            return SCB[:, dc * 128:(dc + 1) * 128]

        def window_proj(evac_fn, wready):
            for ec in range(NDC):
                psi = ec % 4
                w = mm_waits(psi, [("ve", x_cast_done), ("ve", wready)])

                def wmms(e, ec=ec, psi=psi):
                    inst = None
                    for dc in range(NDC):
                        lhsT = wr(dc)[:, ec * 128:(ec + 1) * 128]
                        inst = e.matmul(PS[psi][:, 0:128], lhsT, xw_bf(dc),
                                        start=(dc == 0), stop=(dc == NDC - 1))
                    return inst
                op("tensor", wmms, inc=("te", 1), waits=w)
                evac_fn(ec, psi, cnt("te"))

        def stream_attn_weight(whbm):
            wr_last_use = cnt("te")
            for dc in range(NDC):
                si = ring_load(lambda e, t, dc=dc, whbm=whbm: e.dma_start(
                    out=t[:], in_=whbm[dc * 128:(dc + 1) * 128, :]))
                op("vector", lambda e, si=si, dc=dc: e.tensor_copy(wr(dc), STG[si][:]),
                   inc=("ve", 1), waits=[ring_wait(si), ("te", wr_last_use)])
                ring_done(si, "ve")
            return cnt("ve")

        def attn_proj(mov_fn, evac_fn, wready, mov_ready, slots=(0, 1, 2, 3)):
            for ec in range(NDC):
                psi = slots[ec % len(slots)]
                w = mm_waits(psi, [("ve", wready), ("ve", mov_ready)])

                def mms(e, ec=ec, psi=psi, mov_fn=mov_fn):
                    inst = None
                    for dc in range(NDC):
                        lhsT = wr(dc)[:, ec * 128:(ec + 1) * 128]
                        st, sp = (dc == 0), (dc == NDC - 1)
                        e.matmul(PS[psi][:, 0:512], lhsT, mov_fn(dc)[:, 0:512],
                                 start=st, stop=sp)
                        inst = e.matmul(PS[psi][:, 512:1024], lhsT,
                                        mov_fn(dc)[:, 512:1024], start=st, stop=sp)
                    return inst
                op("tensor", mms, inc=("te", 1), waits=w)
                evac_fn(ec, psi, cnt("te"))

        f_wready = stream_attn_weight(wft)

        def fw(ec):
            return FWB[:, ec * 128:(ec + 1) * 128]

        def fw_evac(ec, psi, tec):
            op("scalar", lambda e, ec=ec, psi=psi: e.copy(fw(ec), PS[psi][:, 0:128]),
               inc=("se", 1), waits=[("te", tec)])
            ps_free[psi] = ("se", cnt("se"))
        window_proj(fw_evac, f_wready)

        def f_evac(ec, psi, tec):
            op("scalar", lambda e, ec=ec, psi=psi: e.copy(b2(ec), PS[psi][:]),
               inc=("se", 1), waits=[("te", tec)])
            ps_free[psi] = ("se", cnt("se"))
        attn_proj(b1, f_evac, f_wready, x_cast_done)

        i_wready = stream_attn_weight(wit)

        def iw_evac(ec, psi, tec):
            s0 = scra(0)
            op("scalar", lambda e, psi=psi, s0=s0: e.activation(
                s0[:, 0:128], PS[psi][:, 0:128], AF.Silu),
                inc=("se", 1), waits=[("te", tec), ("ve", cnt("ve"))])
            ps_free[psi] = ("se", cnt("se"))
            op("scalar", lambda e, ec=ec, s0=s0: e.activation(
                s0[:, 128:256], fw(ec), AF.Sigmoid, scale=-1.0), inc=("se", 1))
            op("scalar", lambda e, ec=ec, s0=s0: e.activation(
                s0[:, 384:512], fw(ec), AF.Sigmoid), inc=("se", 1))
            op("vector", lambda e, s0=s0: e.tensor_mul(
                s0[:, 256:384], s0[:, 0:128], s0[:, 128:256]),
                inc=("ve", 1), waits=[("se", cnt("se"))])
            op("vector", lambda e, s0=s0: e.tensor_tensor_scan(
                s0[:, 512:640], s0[:, 384:512], s0[:, 256:384], 0.0,
                ALU.mult, ALU.add), inc=("ve", 1))
            op("scalar", lambda e, ec=ec, s0=s0: e.copy(
                HINC[:, ec:ec + 1], s0[:, 639:640]),
                inc=("se", 1), waits=[("ve", cnt("ve"))])
        window_proj(iw_evac, i_wready)

        def i_evac(ec, psi, tec):
            op("scalar", lambda e, psi=psi: e.activation(
                scra(0), PS[psi][:], AF.Silu),
                inc=("se", 1), waits=[("te", tec), ("ve", cnt("ve"))])
            ps_free[psi] = ("se", cnt("se"))
            op("scalar", lambda e, ec=ec: e.activation(
                scra(1), b2(ec), AF.Sigmoid, scale=-1.0), inc=("se", 1))
            op("vector", lambda e, ec=ec: e.tensor_mul(b3(ec), scra(0), scra(1)),
               inc=("ve", 1), waits=[("se", cnt("se"))])
            op("scalar", lambda e, ec=ec: e.activation(scra(0), b2(ec), AF.Sigmoid),
               inc=("se", 1), waits=[("ve", cnt("ve"))])
            op("vector", lambda e, ec=ec: e.tensor_copy(scra(1), b3(ec)),
               inc=("ve", 1))
            op("scalar", lambda e, ec=ec: e.activation(
                scra(1)[:, 0:1], scra(0)[:, 0:1], AF.Identity,
                bias=scra(1)[:, 0:1], scale=HINC[:, ec:ec + 1]),
                inc=("se", 1), waits=[("ve", cnt("ve"))])
            op("vector", lambda e, ec=ec: e.tensor_tensor_scan(
                b2(ec), scra(0), scra(1), 0.0, ALU.mult, ALU.add),
                inc=("ve", 1), waits=[("se", cnt("se"))])
        attn_proj(b1, i_evac, i_wready, x_cast_done)
        scan_done = cnt("ve")

        g_wready = stream_attn_weight(wgt)

        def g_evac(ec, psi, tec):
            op("scalar", lambda e, ec=ec, psi=psi: e.activation(
                b3(ec), PS[psi][:], AF.Silu),
                inc=("se", 1), waits=[("te", tec), ("ve", scan_done)])
            ps_free[psi] = ("se", cnt("se"))
            op("vector", lambda e, ec=ec: e.tensor_mul(b3(ec), b2(ec), b3(ec)),
               inc=("ve", 1), waits=[("se", cnt("se"))])
            op("scalar", lambda e, ec=ec: e.activation(
                SCB[:, 0:T], b3(ec), AF.Square), inc=("se", 1),
                waits=[("ve", cnt("ve")), ("te", cnt("te"))])

            def sqmm(e, ec=ec):
                e.matmul(PS[0][0:1, 0:512], ONEC[:], SCB[:, 0:512],
                         start=(ec == 0), stop=(ec == NDC - 1))
                return e.matmul(PS[0][0:1, 512:1024], ONEC[:], SCB[:, 512:1024],
                                start=(ec == 0), stop=(ec == NDC - 1))
            w = [("se", cnt("se")), ("w0", 80)]
            if ec == 0 and ps_free[0] is not None:
                w.append(ps_free[0])
            op("tensor", sqmm, inc=("te", 1), waits=w)
        attn_proj(b1, g_evac, g_wready, x_cast_done, slots=(1, 2, 3))
        g_mm_done = cnt("te")

        o_wready = stream_attn_weight(wot)
        op("vector", lambda e: e.tensor_single_scalar(
            R1V, PS[0][0:1, :], INV_D, ALU.mult),
            inc=("ve", 1), waits=[("te", cnt("te"))])
        ps_free[0] = ("ve", cnt("ve"))
        op("scalar", lambda e: e.activation(R1V, R1V, AF.Sqrt, bias=c_eps5),
           inc=("se", 1), waits=[("ve", cnt("ve")), ("w0", 80)])
        op("vector", lambda e: e.reciprocal(R0, R1V),
           inc=("ve", 1), waits=[("se", cnt("se"))])
        rb_te = bcast(R0, 1, ("ve", cnt("ve")))

        for dc in range(NDC):
            op("vector", lambda e, dc=dc: e.tensor_mul(scra(0), b3(dc), PS[1][:]),
               inc=("ve", 1), waits=[("te", rb_te), ("se", cnt("se")),
                                     ("te", g_mm_done)])
            op("vector", lambda e, dc=dc: e.tensor_single_scalar(
                b1(dc), scra(0), WNS[:, dc:dc + 1], ALU.mult), inc=("ve", 1))
        y_done = cnt("ve")
        ps_free[1] = ("ve", y_done)

        bb_ctx.close()
        XF = ctx.enter_context(nc.sbuf_tensor("XF", [128, NDC * T], F32))

        def xf(dc):
            return XF[:, dc * T:(dc + 1) * T]

        new_block()

        xres_units = [None] * 8

        def load_xres(u):
            xres_units[u] = ring_load(lambda e, t, u=u: e.dma_start(
                out=t[:].rearrange("p (j t) -> p j t", j=2),
                in_=xt[u * 256:(u + 1) * 256, :].rearrange("(j p) t -> p j t", p=128)))

        load_xres(0)
        for ec in range(NDC):
            psi = (1, 2, 3)[ec % 3]
            if ec % 2 == 0 and ec // 2 + 1 < 8:
                load_xres(ec // 2 + 1)
            w = mm_waits(psi, [("ve", o_wready), ("ve", y_done)])

            def omms(e, ec=ec, psi=psi):
                inst = None
                for dc in range(NDC):
                    lhsT = wr(dc)[:, ec * 128:(ec + 1) * 128]
                    st, sp = (dc == 0), (dc == NDC - 1)
                    e.matmul(PS[psi][:, 0:512], lhsT, b1(dc)[:, 0:512],
                             start=st, stop=sp)
                    inst = e.matmul(PS[psi][:, 512:1024], lhsT,
                                    b1(dc)[:, 512:1024], start=st, stop=sp)
                return inst
            op("tensor", omms, inc=("te", 1), waits=w)
            si = xres_units[ec // 2]
            j = ec % 2
            op("vector", lambda e, ec=ec, psi=psi, si=si, j=j: e.tensor_add(
                xf(ec), PS[psi][:], STG[si][:, j * 1024:(j + 1) * 1024]),
                inc=("ve", 1), waits=[("te", cnt("te")), ring_wait(si)])
            ps_free[psi] = ("ve", cnt("ve"))
            if j == 1:
                ring_done(si, "ve")
            op("vector", lambda e, ec=ec: e.tensor_copy(SCB[:, 0:T], xf(ec)),
               inc=("ve", 1))
            op("scalar", lambda e, ec=ec: e.activation(
                SCB[:, T:2 * T], xf(ec), AF.Square),
                inc=("se", 1), waits=[("ve", cnt("ve")), ("te", cnt("te"))])

            def smm1(e, ec=ec):
                st, sp = (ec == 0), (ec == NDC - 1)
                e.matmul(PS[0][0:1, 0:512], ONEC[:], SCB[:, 0:512],
                         start=st, stop=sp)
                e.matmul(PS[0][0:1, 512:1024], ONEC[:], SCB[:, 512:1024],
                         start=st, stop=sp)
                e.matmul(PS[0][32:33, 0:512], ONEC[:], SCB[:, T:T + 512],
                         start=st, stop=sp)
                return e.matmul(PS[0][32:33, 512:1024], ONEC[:],
                                SCB[:, T + 512:2 * T], start=st, stop=sp)
            w2 = [("ve", cnt("ve")), ("se", cnt("se")), ("w0", 80)]
            if ec == 0 and ps_free[0] is not None:
                w2.append(ps_free[0])
            op("tensor", smm1, inc=("te", 1), waits=w2)
        x1_done = cnt("ve")
        x1_stats_te = cnt("te")

        wr_ctx.close()
        xn_ctx = ExitStack()
        XN = xn_ctx.enter_context(
            nc.sbuf_tensor("XN", [128, NDC * T], BF16, side="right"))
        SCI = xn_ctx.enter_context(
            nc.sbuf_tensor("SCI", [128, T], I32, side="right"))

        def xnt(dc):
            return XN[:, dc * T:(dc + 1) * T]

        new_block()

        def ln_stats(src_fn, src_done):
            for dc in range(NDC):
                op("vector", lambda e, dc=dc, src_fn=src_fn: e.tensor_copy(
                    SCB[:, 0:T], src_fn(dc)),
                    inc=("ve", 1), waits=[("ve", src_done), ("te", cnt("te"))])
                op("scalar", lambda e, dc=dc, src_fn=src_fn: e.activation(
                    SCB[:, T:2 * T], src_fn(dc), AF.Square),
                    inc=("se", 1), waits=[("ve", src_done), ("te", cnt("te"))])

                def smm(e, dc=dc):
                    st, sp = (dc == 0), (dc == NDC - 1)
                    e.matmul(PS[0][0:1, 0:512], ONEC[:], SCB[:, 0:512],
                             start=st, stop=sp)
                    e.matmul(PS[0][0:1, 512:1024], ONEC[:], SCB[:, 512:1024],
                             start=st, stop=sp)
                    e.matmul(PS[1][0:1, 0:512], ONEC[:], SCB[:, T:T + 512],
                             start=st, stop=sp)
                    return e.matmul(PS[1][0:1, 512:1024], ONEC[:],
                                    SCB[:, T + 512:2 * T], start=st, stop=sp)
                w = [("ve", cnt("ve")), ("se", cnt("se")), ("w0", 80)]
                if dc == 0:
                    for psi in (0, 1):
                        if ps_free[psi] is not None:
                            w.append(ps_free[psi])
                op("tensor", smm, inc=("te", 1), waits=w)

        def ln_rows(eps_ap, sum_row=None, sq_row=None):
            if sum_row is None:
                sum_row, sq_row = PS[0][0:1, :], PS[1][0:1, :]
            tec = cnt("te")
            op("vector", lambda e, sum_row=sum_row: e.tensor_single_scalar(
                R0, sum_row, INV_D, ALU.mult), inc=("ve", 1),
                waits=[("te", tec)])
            ps_free[0] = ("ve", cnt("ve"))
            op("vector", lambda e: e.tensor_mul(SCROW, R0, R0), inc=("ve", 1))
            op("vector", lambda e, sq_row=sq_row: e.scalar_tensor_tensor(
                R1V, sq_row, INV_D, SCROW, ALU.mult, ALU.subtract),
                inc=("ve", 1), waits=[("te", tec)])
            ps_free[1] = ("ve", cnt("ve"))
            op("scalar", lambda e, eps_ap=eps_ap: e.activation(
                R1V, R1V, AF.Sqrt, bias=eps_ap),
                inc=("se", 1), waits=[("ve", cnt("ve"))])
            op("vector", lambda e: e.reciprocal(SCROW, R1V),
               inc=("ve", 1), waits=[("se", cnt("se"))])
            op("vector", lambda e: e.tensor_single_scalar(
                R0, R0, -1.0, ALU.mult), inc=("ve", 1))

        ln_rows(c_eps5, PS[0][0:1, :], PS[0][32:33, :])
        bcast(R0, 2, ("ve", cnt("ve")))
        bc1 = bcast(SCROW, 3, ("ve", cnt("ve")))
        for dc in range(NDC):
            op("vector", lambda e, dc=dc: e.tensor_add(xf(dc), xf(dc), PS[2][:]),
               inc=("ve", 1), waits=[("te", bc1)])
            op("vector", lambda e, dc=dc: e.tensor_mul(xf(dc), xf(dc), PS[3][:]),
               inc=("ve", 1))
        x2_done = cnt("ve")
        ps_free[2] = ("ve", x2_done)
        ps_free[3] = ("ve", x2_done)

        for j in range(2):
            op("gpsimd", lambda e, j=j: e.dma_start(
                out=x2sp[j * 1024:(j + 1) * 1024, :].rearrange(
                    "(k p) t -> p k t", p=128),
                in_=XF[:, j * 8 * T:(j + 1) * 8 * T].rearrange(
                    "p (k t) -> p k t", k=8)),
                inc=("gd", 16), waits=[("ve", x2_done)])
        x2_spill = cnt("gd")

        ln_stats(xf, x2_done)
        ln_rows(c_eps8)
        bcast(R0, 2, ("ve", cnt("ve")))
        bc2 = bcast(SCROW, 3, ("ve", cnt("ve")))
        for dc in range(NDC):
            op("vector", lambda e, dc=dc: e.tensor_add(scra(0), xf(dc), PS[2][:]),
               inc=("ve", 1), waits=[("te", bc2)])
            op("vector", lambda e, dc=dc: e.tensor_mul(xnt(dc), scra(0), PS[3][:]),
               inc=("ve", 1))
        xn_done = cnt("ve")
        ps_free[2] = ("ve", xn_done)
        ps_free[3] = ("ve", xn_done)

        for dc in range(NDC):
            op("scalar", lambda e, dc=dc: e.activation(
                SCB[:, 0:T], xnt(dc), AF.Abs),
                inc=("se", 1), waits=[("ve", cnt("ve")), ("ve", xn_done)])
            if dc == 0:
                op("vector", lambda e: e.tensor_copy(scra(0), SCB[:, 0:T]),
                   inc=("ve", 1), waits=[("se", cnt("se"))])
            else:
                op("vector", lambda e: e.tensor_max(scra(0), scra(0), SCB[:, 0:T]),
                   inc=("ve", 1), waits=[("se", cnt("se"))])
        tree_done = cnt("ve")
        for j in range(8):
            psi = j % 4
            op("tensor", lambda e, j=j, psi=psi: e.transpose(
                PS[psi][:, 0:128], scra(0)[:, j * 128:(j + 1) * 128], IDENT[:]),
                inc=("te", 1), waits=mm_waits(psi, [("ve", tree_done), ("w0", 80)]))
            op("vector", lambda e, j=j, psi=psi: e.reduce_max(
                scra(1)[:, j:j + 1], PS[psi][:, 0:128], axis=AX.X),
                inc=("ve", 1), waits=[("te", cnt("te"))])
            ps_free[psi] = ("ve", cnt("ve"))
        mc_done = cnt("ve")
        for j in range(8):
            psi = j % 4
            op("tensor", lambda e, j=j, psi=psi: e.transpose(
                PS[psi][0:1, 0:128], scra(1)[:, j:j + 1], IDENT[:]),
                inc=("te", 1), waits=mm_waits(psi, [("ve", mc_done)]))
            op("vector", lambda e, j=j, psi=psi: e.tensor_copy(
                R0[:, j * 128:(j + 1) * 128], PS[psi][0:1, 0:128]),
                inc=("ve", 1), waits=[("te", cnt("te"))])
            ps_free[psi] = ("ve", cnt("ve"))
        op("vector", lambda e: e.reciprocal(R1V, R0), inc=("ve", 1))
        op("vector", lambda e: e.tensor_single_scalar(
            R1V, R1V, 127.0, ALU.mult), inc=("ve", 1))
        op("vector", lambda e: e.tensor_single_scalar(
            R0, R0, INV127, ALU.mult), inc=("ve", 1))
        bcast(R1V, 2, ("ve", cnt("ve")))
        sb_te = bcast(R0, 3, ("ve", cnt("ve")))
        sb_done = cnt("ve")

        for dc in range(NDC):
            op("vector", lambda e, dc=dc: e.tensor_mul(PS[0][:], xnt(dc), PS[2][:]),
               inc=("ve", 1),
               waits=mm_waits(0, [("te", sb_te), ("ve", sb_done)] if dc == 0 else []))
            op("vector", lambda e: e.tensor_copy(SCI[:], PS[0][:]), inc=("ve", 1))
            op("scalar", lambda e: e.copy(scra(0), SCI[:]),
               inc=("se", 1), waits=[("ve", cnt("ve"))])
            op("vector", lambda e: e.tensor_sub(PS[1][:], PS[0][:], scra(0)),
               inc=("ve", 1),
               waits=[("se", cnt("se"))] + (mm_waits(1, []) if dc == 0 else []))
            op("vector", lambda e: e.scalar_tensor_tensor(
                scra(0), PS[1][:], 0.5, scra(0), ALU.is_ge, ALU.add),
                inc=("ve", 1))
            op("vector", lambda e: e.scalar_tensor_tensor(
                PS[0][:], PS[1][:], -0.5, scra(0), ALU.is_le, ALU.subtract),
                inc=("ve", 1))
            op("vector", lambda e: e.tensor_scalar(
                scra(0), PS[0][:], -1.0, 127.0, ALU.mult, ALU.min), inc=("ve", 1))
            op("vector", lambda e, dc=dc: e.scalar_tensor_tensor(
                b1(dc), scra(0), -128.0, PS[3][:], ALU.max, ALU.mult),
                inc=("ve", 1))
        q_done = cnt("ve")
        ps_free[0] = ("ve", q_done)
        ps_free[1] = ("ve", q_done)
        ps_free[2] = ("ve", q_done)
        ps_free[3] = ("ve", q_done)

        xn_ctx.close()
        ffn_ctx = ExitStack()
        SGN = ffn_ctx.enter_context(
            nc.sbuf_tensor("SGN", [128, 3 * 8192], BF16, side="right"))
        PB = ffn_ctx.enter_context(
            nc.sbuf_tensor("PB", [128, NHH * T], BF16, side="right"))

        def pb(hm):
            return PB[:, hm * T:(hm + 1) * T]

        def sgn_blk(s):
            return SGN[:, s * 8192:(s + 1) * 8192]

        sgn_war = [None, None, None]
        acc_started = False
        gbi = 0

        new_block()
        prev_down_te = 0
        for hh in range(NHH):
            gbufs = [scra(0), scra(1), SCB[:, 0:T], SCB[:, T:2 * T]]
            hb0_u_te = None
            for hb in range(2):
                gslot = gbi % 3
                gbi += 1
                gsig = []
                for u in range(4):
                    si = ring_load(lambda e, t, hh=hh, hb=hb, u=u: e.dma_start(
                        out=t[:].rearrange("p (j h) -> p j h", j=4),
                        in_=wgq[hh, u * 512:(u + 1) * 512,
                                hb * 512:(hb + 1) * 512].rearrange(
                                    "(j p) h -> p j h", p=128)))
                    w = [ring_wait(si)]
                    if sgn_war[gslot] is not None:
                        w.append(sgn_war[gslot])
                    op("scalar", lambda e, si=si, gslot=gslot, u=u: e.sign(
                        sgn_blk(gslot)[:, u * 2048:(u + 1) * 2048], STG[si][:]),
                        inc=("se", 1), waits=w)
                    ring_done(si, "se")
                gready = cnt("se")
                uslot = gbi % 3
                gbi += 1
                usig_done = None
                for hml in range(4):
                    psg = 0 if (hml % 2 == 0) else 2

                    def g_chain(e, hml=hml, gslot=gslot, psg=psg):
                        inst = None
                        for dc in range(NDC):
                            lhsT = sgn_blk(gslot)[
                                :, dc * 512 + hml * 128: dc * 512 + (hml + 1) * 128]
                            st, sp = (dc == 0), (dc == NDC - 1)
                            e.matmul(PS[psg][:, 0:512], lhsT, b1(dc)[:, 0:512],
                                     start=st, stop=sp)
                            inst = e.matmul(PS[psg][:, 512:1024], lhsT,
                                            b1(dc)[:, 512:1024], start=st, stop=sp)
                        return inst
                    op("tensor", g_chain, inc=("te", 1),
                       waits=mm_waits(psg, [("se", gready), ("ve", q_done)]))
                    sgn_war[gslot] = ("te", cnt("te"))
                    op("scalar", lambda e, hml=hml, psg=psg: e.activation(
                        gbufs[hml], PS[psg][:], AF.Silu),
                        inc=("se", 1), waits=[("te", cnt("te")), ("ve", cnt("ve"))])
                    ps_free[psg] = ("se", cnt("se"))
                    si = ring_load(lambda e, t, hh=hh, hb=hb, u=hml: e.dma_start(
                        out=t[:].rearrange("p (j h) -> p j h", j=4),
                        in_=wuq[hh, u * 512:(u + 1) * 512,
                                hb * 512:(hb + 1) * 512].rearrange(
                                    "(j p) h -> p j h", p=128)))
                    w = [ring_wait(si)]
                    if sgn_war[uslot] is not None and hml == 0:
                        w.append(sgn_war[uslot])
                    op("scalar", lambda e, si=si, uslot=uslot, u=hml: e.sign(
                        sgn_blk(uslot)[:, u * 2048:(u + 1) * 2048], STG[si][:]),
                        inc=("se", 1), waits=w)
                    ring_done(si, "se")
                    usig_done = cnt("se")
                for hml in range(4):
                    hm = hb * 4 + hml
                    psu = 1 if (hml % 2 == 0) else 3

                    def u_chain(e, hml=hml, uslot=uslot, psu=psu):
                        inst = None
                        for dc in range(NDC):
                            lhsT = sgn_blk(uslot)[
                                :, dc * 512 + hml * 128: dc * 512 + (hml + 1) * 128]
                            st, sp = (dc == 0), (dc == NDC - 1)
                            e.matmul(PS[psu][:, 0:512], lhsT, b1(dc)[:, 0:512],
                                     start=st, stop=sp)
                            inst = e.matmul(PS[psu][:, 512:1024], lhsT,
                                            b1(dc)[:, 512:1024], start=st, stop=sp)
                        return inst
                    op("tensor", u_chain, inc=("te", 1),
                       waits=mm_waits(psu, [("se", usig_done)]))
                    sgn_war[uslot] = ("te", cnt("te"))
                    op("vector", lambda e, hm=hm, hml=hml, psu=psu: e.tensor_mul(
                        pb(hm), gbufs[hml], PS[psu][:]),
                        inc=("ve", 1), waits=[("te", cnt("te")), ("se", cnt("se")),
                                              ("te", prev_down_te)])
                    ps_free[psu] = ("ve", cnt("ve"))
                if hb == 0:
                    hb0_u_te = cnt("te")
            p_done = cnt("ve")

            nxt = gbi % 3
            sl_slots = [(nxt + 1) % 3, (nxt + 2) % 3]
            slice_aps = []
            for k in range(8):
                si = ring_load(lambda e, t, hh=hh, k=k: e.dma_start(
                    out=t[:],
                    in_=wdt[hh * HH + k * 128: hh * HH + (k + 1) * 128, :]))
                sslot = sl_slots[k // 4]
                dst = sgn_blk(sslot)[:, (k % 4) * 2048:((k % 4) + 1) * 2048]
                slice_aps.append(dst)
                w = [ring_wait(si), ("te", hb0_u_te)]
                if sgn_war[sslot] is not None:
                    w.append(sgn_war[sslot])
                op("scalar", lambda e, si=si, dst=dst: e.sign(dst, STG[si][:]),
                   inc=("se", 1), waits=w)
                ring_done(si, "se")
            sdown_done = cnt("se")
            for sslot in sl_slots:
                sgn_war[sslot] = None

            for dc in range(NDC):
                psi = dc % 4

                def dmms(e, psi=psi, dc=dc, saps=list(slice_aps)):
                    inst = None
                    for k in range(8):
                        lhsT = saps[k][:, dc * 128:(dc + 1) * 128]
                        st, sp = (k == 0), (k == 7)
                        e.matmul(PS[psi][:, 0:512], lhsT, pb(k)[:, 0:512],
                                 start=st, stop=sp)
                        inst = e.matmul(PS[psi][:, 512:1024], lhsT,
                                        pb(k)[:, 512:1024], start=st, stop=sp)
                    return inst
                op("tensor", dmms, inc=("te", 1),
                   waits=mm_waits(psi, [("se", sdown_done), ("ve", p_done)]))
                tec = cnt("te")
                if not acc_started:
                    op("vector", lambda e, dc=dc, psi=psi: e.tensor_copy(
                        xf(dc), PS[psi][:]),
                        inc=("ve", 1), waits=[("te", tec), ("gd", x2_spill)])
                else:
                    op("vector", lambda e, dc=dc, psi=psi: e.tensor_add(
                        xf(dc), xf(dc), PS[psi][:]),
                        inc=("ve", 1), waits=[("te", tec)])
                ps_free[psi] = ("ve", cnt("ve"))
            acc_started = True
            down_te = cnt("te")
            prev_down_te = down_te
            for sslot in sl_slots:
                sgn_war[sslot] = ("te", down_te)
        acc_done = cnt("ve")

        ffn_ctx.close()
        e_ctx = ExitStack()
        OST = [e_ctx.enter_context(
            nc.sbuf_tensor(f"OST{i}", [128, 2048], F32, side="right"))
            for i in range(3)]
        NMRC = e_ctx.enter_context(
            nc.sbuf_tensor("NMRC", [128, 8], F32, side="right"))
        RSTC = e_ctx.enter_context(
            nc.sbuf_tensor("RSTC", [128, 8], F32, side="right"))

        new_block()

        for u in range(8):
            si = ring_load(lambda e, t, u=u: e.dma_start(
                out=t[:].rearrange("p (j t) -> p j t", j=2),
                in_=x2sp[u * 256:(u + 1) * 256, :].rearrange(
                    "(j p) t -> p j t", p=128)),
                extra_waits=[("gd", x2_spill)] if u == 0 else ())
            for j in range(2):
                dc = 2 * u + j
                op("vector", lambda e, si=si, j=j, dc=dc: e.tensor_add(
                    xf(dc), xf(dc), STG[si][:, j * 1024:(j + 1) * 1024]),
                    inc=("ve", 1), waits=[ring_wait(si), ("ve", acc_done)])
            ring_done(si, "ve")
        v_done = cnt("ve")

        ln_stats(xf, v_done)
        ln_rows(c_eps5)
        op("vector", lambda e: e.tensor_mul(R0, R0, SCROW), inc=("ve", 1))
        rows_done = cnt("ve")

        for tb in range(8):
            psi = tb % 4

            def rtr(e, tb=tb, psi=psi):
                e.transpose(PS[psi][:, 0:1], R0[:, tb * 128:(tb + 1) * 128],
                            IDENT[0:1, 0:1])
                return e.transpose(PS[psi][:, 512:513],
                                   SCROW[:, tb * 128:(tb + 1) * 128],
                                   IDENT[0:1, 0:1])
            op("tensor", rtr, inc=("te", 1),
               waits=mm_waits(psi, [("ve", rows_done)]))
            op("vector", lambda e, tb=tb, psi=psi: e.tensor_copy(
                NMRC[:, tb:tb + 1], PS[psi][:, 0:1]),
                inc=("ve", 1), waits=[("te", cnt("te"))])
            op("vector", lambda e, tb=tb, psi=psi: e.tensor_copy(
                RSTC[:, tb:tb + 1], PS[psi][:, 512:513]), inc=("ve", 1))
            ps_free[psi] = ("ve", cnt("ve"))
        cols_done = cnt("ve")

        ofill = [0, 0, 0]
        for tb in range(8):
            slot = tb % 3
            for dc in range(NDC):
                psi = dc % 4
                op("tensor", lambda e, tb=tb, dc=dc, psi=psi: e.transpose(
                    PS[psi][:, 0:128], xf(dc)[:, tb * 128:(tb + 1) * 128],
                    IDENT[:]), inc=("te", 1),
                    waits=mm_waits(psi, [("ve", cols_done)]))
                w = [("te", cnt("te"))]
                if dc == 0 and ofill[slot] > 0:
                    w.append((f"o{slot}", 16 * ofill[slot]))
                op("scalar", lambda e, tb=tb, dc=dc, psi=psi, slot=slot:
                   e.activation(OST[slot][:, dc * 128:(dc + 1) * 128],
                                PS[psi][:, 0:128], AF.Identity,
                                bias=NMRC[:, tb:tb + 1],
                                scale=RSTC[:, tb:tb + 1]),
                   inc=("se", 1), waits=w)
                ps_free[psi] = ("se", cnt("se"))
            op("gpsimd", lambda e, tb=tb, slot=slot: e.dma_start(
                out=out[tb * 128:(tb + 1) * 128, :], in_=OST[slot][:]),
                inc=(f"o{slot}", 16), waits=[("se", cnt("se"))])
            ofill[slot] += 1

        for slot in range(3):
            op("gpsimd", lambda e: None,
               waits=[(f"o{slot}", 16 * ofill[slot])])

        for recs in blocks:
            with nc.Block() as blk:
                def make_body(engname, recs=recs):
                    def body(eng):
                        last_wait = {}
                        for (ename, waits, fn, inc) in recs:
                            if ename != engname:
                                continue
                            for (s, v) in waits:
                                if last_wait.get(s, 0) < v:
                                    eng.wait_ge(sems[s], v)
                                    last_wait[s] = v
                            inst = fn(eng)
                            if inc is not None and inst is not None:
                                inst.then_inc(sems[inc[0]], inc[1])
                    return body
                blk.sync(make_body("sync"))
                blk.tensor(make_body("tensor"))
                blk.vector(make_body("vector"))
                blk.scalar(make_body("scalar"))
                blk.gpsimd(make_body("gpsimd"))

        e_ctx.close()
    return nc


_NC_CACHE = {}


def _get_nc():
    if "nc" not in _NC_CACHE:
        _NC_CACHE["nc"] = build_nc()
    return _NC_CACHE["nc"]


def kernel(L0_attn_x, L0_attn_Wi, L0_attn_Wf, L0_attn_Wg, L0_attn_w_norm,
           L0_attn_Wo, L0_ffn_W_g, L0_ffn_W_u, L0_ffn_W_d):
    x = np.asarray(L0_attn_x, dtype=np.float32).reshape(8192, D)
    xT = np.ascontiguousarray(x.T)
    wft = np.ascontiguousarray(np.asarray(L0_attn_Wf, np.float32).T)
    wit = np.ascontiguousarray(np.asarray(L0_attn_Wi, np.float32).T)
    wgt = np.ascontiguousarray(np.asarray(L0_attn_Wg, np.float32).T)
    wot = np.ascontiguousarray(np.asarray(L0_attn_Wo, np.float32).T)
    wgq = np.ascontiguousarray(
        np.asarray(L0_ffn_W_g, np.float32).T.reshape(D, NHH, HH).transpose(1, 0, 2))
    wuq = np.ascontiguousarray(
        np.asarray(L0_ffn_W_u, np.float32).T.reshape(D, NHH, HH).transpose(1, 0, 2))
    wdt = np.ascontiguousarray(np.asarray(L0_ffn_W_d, np.float32).T)
    wn = np.ascontiguousarray(
        np.asarray(L0_attn_w_norm, np.float32).reshape(NDC, 128).T)
    onec = np.ones((128, 1), dtype=ml_dtypes.bfloat16)
    oner = np.ones((1, 128), dtype=np.float32)
    ident = np.eye(128, dtype=np.float32)
    consts = np.tile(np.array([[INV_D, 1e-5, 1e-8, 0.0]], np.float32), (128, 1))

    shared = dict(wft=wft, wit=wit, wgt=wgt, wot=wot, wgq=wgq, wuq=wuq,
                  wdt=wdt, wn=wn, onec=onec, oner=oner, ident=ident,
                  consts=consts)
    in_maps = []
    for c in range(NCORES):
        m = dict(shared)
        m["xt"] = np.ascontiguousarray(xT[:, c * T:(c + 1) * T])
        if c == 0:
            m["xw"] = np.zeros((D, 128), dtype=np.float32)
        else:
            m["xw"] = np.ascontiguousarray(xT[:, c * T - 128:c * T])
        in_maps.append(m)

    nc = _get_nc()
    res = run_bass_kernel_spmd(nc, in_maps, core_ids=list(range(NCORES)))
    outs = [np.asarray(r["out"], dtype=np.float32) for r in res.results]
    return np.concatenate(outs, axis=0).reshape(1, 8192, D)
